# version 01 (final)
# baseline (speedup 1.0000x reference)
import bisect
import numpy as np
import concourse.bass as bass
import concourse.mybir as mybir
from concourse.bass_utils import run_bass_kernel_spmd

F32 = mybir.dt.float32
BF16 = mybir.dt.bfloat16
I32 = mybir.dt.int32
AF = mybir.ActivationFunctionType
ALU = mybir.AluOpType
AX = mybir.AxisListType

D = 1024
T = 4096
KT = D // 128
NCORE = 8
TS = 512
NSLC = T // TS

QUEUES = ("pe", "dve", "act", "pool", "sp")
ENG_ATTR = {"pe": "tensor", "dve": "vector", "act": "scalar", "pool": "gpsimd", "sp": "sync"}


class Prog:
    def __init__(self, nc):
        self.nc = nc
        self.items = {q: [] for q in QUEUES}
        self.epoch = 0
        self.counts = {(0, q): [0, 0] for q in QUEUES}
        self.last_w = {}
        self.readers = {}
        self.host_inputs = {}
        self.open_group = None

    def new_epoch(self):
        self.epoch += 1
        for q in QUEUES:
            self.counts[(self.epoch, q)] = [0, 0]

    def input(self, name, arr, dtype=None):
        if name in self.host_inputs:
            return self.host_inputs[name][1]
        a = np.ascontiguousarray(arr)
        if dtype is None:
            dtype = {np.dtype(np.float32): F32, np.dtype(np.int32): I32}[a.dtype]
        if dtype == BF16 and a.dtype != "bfloat16":
            import ml_dtypes
            a = a.astype(ml_dtypes.bfloat16)
        ap = self.nc.dram_tensor(name, list(a.shape), dtype, kind="ExternalInput").ap()
        self.host_inputs[name] = (a, ap)
        return ap

    def op(self, q, emit, rd=(), wr=(), dma=False):
        kind = 1 if dma else 0
        ep = self.epoch
        idx = self.counts[(ep, q)][kind]
        me = (ep, q, kind, idx)
        deps = set()
        for k in rd:
            w = self.last_w.get(k)
            if w is not None:
                deps.add(w)
            self.readers.setdefault(k, []).append(me)
        for k in wr:
            w = self.last_w.get(k)
            if w is not None:
                deps.add(w)
            for r in self.readers.get(k, ()):
                deps.add(r)
            self.readers[k] = []
            self.last_w[k] = me
        deps = [d for d in deps if d != me and not (q == "pe" and d[1] == "pe" and d[2] == 0 and kind == 0)]
        self.items[q].append({"emit": emit, "deps": deps, "kind": kind, "idx": idx, "ep": ep})
        self.counts[(ep, q)][kind] += 1

    def fence(self, q, rd):
        self.op(q, None, rd=rd)

    def barrier(self, q, keys):
        self.op(q, lambda e: e.nop(), wr=keys)

    def mega_barrier(self):
        keys = set(self.last_w.keys()) | set(self.readers.keys())
        self.op("pool", lambda e: e.nop(), wr=sorted(keys))

    def dma(self, q, out, in_, rd=(), wr=()):
        self.op(q, lambda e, o=out, i=in_: e.dma_start(out=o, in_=i), rd=rd, wr=wr, dma=True)

    def mm(self, out, lhsT, rhs, start, stop, rd=(), wr=(), tile_position=None):
        gk = wr[0] if wr else None
        if start and not stop:
            assert self.open_group is None, f"matmul group {gk} opened inside open group {self.open_group}"
            self.open_group = gk
        elif self.open_group is not None:
            assert gk == self.open_group, f"matmul {gk} interleaved into open group {self.open_group}"
            if stop:
                self.open_group = None
        self.op("pe", lambda e, o=out, l=lhsT, r=rhs, s0=start, s1=stop, tp=tile_position:
                e.matmul(o, l, r, start=s0, stop=s1, tile_position=tp), rd=rd, wr=wr)

    def trans(self, out, in_, ident, rd=(), wr=()):
        assert self.open_group is None, f"transpose inside open matmul group {self.open_group}"
        self.op("pe", lambda e, o=out, i=in_, d=ident: e.transpose(o, i, d), rd=rd, wr=wr)

    def act(self, out, in_, func, bias=0.0, scale=1.0, rd=(), wr=(), q="act"):
        self.op(q, lambda e, o=out, i=in_, f=func, b=bias, s=scale:
                e.activation(out=o, in_=i, func=f, bias=b, scale=s), rd=rd, wr=wr)

    def tt(self, out, in0, in1, alu, rd=(), wr=(), q="dve"):
        self.op(q, lambda e, o=out, a=in0, b=in1, op=alu: e.tensor_tensor(out=o, in0=a, in1=b, op=op),
                rd=rd, wr=wr)

    def ts(self, out, in0, s1, s2, op0, op1=None, rd=(), wr=(), q="dve"):
        if s2 is None and op1 is None:
            if isinstance(s1, (int, float)):
                self.op(q, lambda e, o=out, a=in0, s=s1, op=op0:
                        e.tensor_single_scalar(out=o, in_=a, scalar=s, op=op),
                        rd=rd, wr=wr)
            else:
                self.op(q, lambda e, o=out, a=in0, x1=s1, o0=op0:
                        e.tensor_scalar(out=o, in0=a, scalar1=x1, scalar2=0.0, op0=o0, op1=ALU.add),
                        rd=rd, wr=wr)
        else:
            self.op(q, lambda e, o=out, a=in0, x1=s1, x2=s2, o0=op0, o1=op1:
                    e.tensor_scalar(out=o, in0=a, scalar1=x1, scalar2=x2, op0=o0, op1=o1),
                    rd=rd, wr=wr)

    def stt(self, out, in0, scalar, in1, op0, op1, rd=(), wr=(), q="dve"):
        self.op(q, lambda e, o=out, a=in0, s=scalar, b=in1, o0=op0, o1=op1:
                e.scalar_tensor_tensor(out=o, in0=a, scalar=s, in1=b, op0=o0, op1=o1),
                rd=rd, wr=wr)

    def copy(self, out, in_, rd=(), wr=(), q="dve"):
        self.op(q, lambda e, o=out, i=in_: e.tensor_copy(out=o, in_=i), rd=rd, wr=wr)

    def memset(self, ap, v, rd=(), wr=(), q="dve"):
        self.op(q, lambda e, a=ap, x=v: e.memset(a, x), rd=rd, wr=wr)

    def red(self, out, in_, alu, rd=(), wr=(), q="dve", negate=False):
        self.op(q, lambda e, o=out, i=in_, op=alu, n=negate: e.tensor_reduce(out=o, in_=i, axis=AX.X, op=op, negate=n),
                rd=rd, wr=wr)

    def scan(self, out, data0, data1, initial, rd=(), wr=(), q="dve"):
        self.op(q, lambda e, o=out, a=data0, b=data1, i=initial:
                e.tensor_tensor_scan(out=o, data0=a, data1=b, initial=i, op0=ALU.mult, op1=ALU.add),
                rd=rd, wr=wr)

    def recip(self, out, in_, rd=(), wr=()):
        self.op("dve", lambda e, o=out, i=in_: e.reciprocal(out=o, in_=i), rd=rd, wr=wr)

    def finalize(self):
        nc = self.nc
        tsets = {}
        for q in QUEUES:
            for it in self.items[q]:
                for d in it["deps"]:
                    tsets.setdefault((d[0], d[1], d[2]), set()).add(d[3])
        targets = {k: sorted(v) for k, v in tsets.items()}
        sem_keys = set(targets.keys())
        for (ep, q), cnts in self.counts.items():
            if cnts[1] > 0:
                sem_keys.add((ep, q, 1))
        sems = {}
        import contextlib
        stack = contextlib.ExitStack()
        for k in sorted(sem_keys):
            sems[k] = stack.enter_context(nc.semaphore(f"s{k[0]}_{k[1]}_{k[2]}"))

        def val(ep2, q2, k2, i2):
            if k2 == 1:
                return (i2 + 1) * 16
            return bisect.bisect_right(targets[(ep2, q2, k2)], i2)

        with stack:
            with nc.Block() as block:
                for q in QUEUES:
                    items = self.items[q]
                    if not items:
                        continue

                    def runner(eng, items=items, q=q):
                        water = {}
                        for it in items:
                            byq = {}
                            for d in it["deps"]:
                                wk = (d[0], d[1], d[2])
                                if d[3] > byq.get(wk, -1):
                                    byq[wk] = d[3]
                            for wk in sorted(byq):
                                v = val(wk[0], wk[1], wk[2], byq[wk])
                                if water.get(wk, 0) >= v:
                                    continue
                                eng.wait_ge(sems[wk], v)
                                water[wk] = v
                            if it["emit"] is not None:
                                instr = it["emit"](eng)
                                key = (it["ep"], q, it["kind"])
                                if it["kind"] == 1:
                                    instr.then_inc(sems[key], 16)
                                elif key in tsets and it["idx"] in tsets[key]:
                                    instr.then_inc(sems[key], 1)
                    getattr(block, ENG_ATTR[q])(runner)
        return nc


class Ctx:
    pass


def make_ctx(nc, P, consts):
    c = Ctx()
    c.A = nc.alloc_sbuf_tensor("arenaA", [128, 8 * T], BF16)
    c.B = nc.alloc_sbuf_tensor("arenaB", [128, 8 * T], BF16)
    c.W1 = nc.alloc_sbuf_tensor("arenaW1", [128, 12288], BF16)
    c.W2 = nc.alloc_sbuf_tensor("arenaW2", [128, 12288], BF16)
    c.ET = [nc.alloc_sbuf_tensor(f"et{i}", [128, TS], F32) for i in range(2)]
    c.EB = [nc.alloc_sbuf_tensor(f"eb{i}", [128, TS], BF16) for i in range(2)]
    c.GR = nc.alloc_sbuf_tensor("grows", [128, 2 * TS], F32)
    c.CC = nc.alloc_sbuf_tensor("constcols", [128, 8], F32)
    c.CARR = nc.alloc_sbuf_tensor("carr", [128, 16], F32)
    c.EBI = nc.alloc_sbuf_tensor("ebi", [128, 16], F32)
    c.cc_cache = {}
    c.P = P

    def ccol(val):
        if val not in c.cc_cache:
            idx = len(c.cc_cache)
            assert idx < 8
            P.memset(c.CC[:, idx:idx + 1], float(val), wr=[f"cc{idx}"], q="dve")
            c.cc_cache[val] = (c.CC[:, idx:idx + 1], f"cc{idx}")
        return c.cc_cache[val]
    c.ccol = ccol
    c._params = {}

    def sbuf_param(P_, name, dram_ap, shape, dtype=F32):
        if name not in c._params:
            tns = nc.alloc_sbuf_tensor("p_" + name, list(shape), dtype)
            P_.dma("pool" if dtype == BF16 else "sp", tns[:], dram_ap, wr=[f"p{name}"])
            c._params[name] = tns
        return c._params[name][:]
    c.sbuf_param = sbuf_param
    c.CMB = nc.alloc_sbuf_tensor("cmb", [128, 512], F32)
    c.PS = [nc.alloc_psum_tensor(f"ps{i}", [128, 512], F32) for i in range(6)]
    c.PTF = nc.alloc_psum_tensor("ptf", [128, 128], F32)
    c.PTB = nc.alloc_psum_tensor("ptb", [128, 512], BF16)
    c.ones_col_b = P.input("c_ones_col", consts["ones128x1"], BF16)[:]
    c.ones_col_f = P.input("c_ones_col_f", consts["ones128x1f"])[:]
    c.ones_row_f = P.input("c_ones_row_f", consts["ones1x128f"])[:]
    c.ones_row_b = P.input("c_ones_row_b", consts["ones1x128"], BF16)[:]
    c.idb = P.input("c_idb", consts["id128"], BF16)[:]
    c.idf = P.input("c_idf", consts["id128f"])[:]
    c.s_ones_col_b = nc.alloc_sbuf_tensor("s_ones_col_b", [128, 1], BF16)
    c.s_ones_col_f = nc.alloc_sbuf_tensor("s_ones_col_f", [128, 1], F32)
    c.s_ones_row_f = nc.alloc_sbuf_tensor("s_ones_row_f", [1, 128], F32)
    c.s_idb = nc.alloc_sbuf_tensor("s_idb", [128, 128], BF16)
    P.dma("sp", c.s_ones_col_b[:], c.ones_col_b, wr=["c1"])
    P.dma("sp", c.s_ones_col_f[:], c.ones_col_f, wr=["c2"])
    P.dma("sp", c.s_ones_row_f[:], c.ones_row_f, wr=["c3"])
    P.dma("sp", c.s_idb[:], c.idb, wr=["c4"])
    c.const_keys = ["c1", "c2", "c3", "c4"]
    return c


def base_consts():
    return {
        "ones128x1": np.ones((128, 1), np.float32),
        "ones128x1f": np.ones((128, 1), np.float32),
        "ones1x128f": np.ones((1, 128), np.float32),
        "ones1x128": np.ones((1, 128), np.float32),
        "id128": np.eye(128, dtype=np.float32),
        "id128f": np.eye(128, dtype=np.float32),
    }


def acol(c, i, lo, n):
    return c.A[:, i * T + lo: i * T + lo + n]


def bcol(c, i, lo, n):
    return c.B[:, i * T + lo: i * T + lo + n]


def b_f32(c, lo, n):
    return c.B[:].bitcast(F32)[:, lo: lo + n]


def hbm_cm_slice(x_ap, t0, n):
    return x_ap[:, t0:t0 + n].rearrange("(k p) t -> p k t", p=128)


def emit_ln_slice(P, c, x_ap, t0, n, xs_f32, xs_keys, out_tiles, out_keys, tag,
                  y_hbm=None, gcol=None, bcol_=None, gb_keys=(), src_tiles=None, src_keys=None,
                  y_hbm_plain=None):
    if src_tiles is None:
        P.dma("sp", xs_f32.rearrange("p (k t) -> p k t", k=KT), hbm_cm_slice(x_ap, t0, n),
              rd=["xres"], wr=list(xs_keys))
        src = lambda k: xs_f32[:, k * n:(k + 1) * n]
        skeys = lambda k: list(xs_keys)
        src_is_f32 = True
    else:
        src = lambda k: src_tiles[k]
        skeys = lambda k: [src_keys[k]]
        src_is_f32 = False
    mu = c.GR[0:1, 0:n]
    rstd = c.GR[0:1, TS:TS + n]
    tmp = c.ET[0][0:1, 0:n]
    for k in range(KT):
        if src_is_f32:
            xb = c.EB[k % 2][:, 0:n]
            P.copy(xb, src(k), rd=skeys(k), wr=[f"EB{k%2}"])
            xbk = f"EB{k%2}"
        else:
            xb = src(k)
            xbk = src_keys[k]
        P.mm(c.PS[4][0:1, 0:n], c.s_ones_col_b[:], xb, start=(k == 0), stop=(k == KT - 1),
             rd=[xbk, "c1"], wr=["PS4"])
    for k in range(KT):
        sq = c.ET[1][:].bitcast(BF16)[:, 0:n] if k % 2 == 0 else c.EB[k % 2][:, 0:n]
        sqk = "ET1" if k % 2 == 0 else f"EB{k%2}"
        P.act(sq, src(k), AF.Square, rd=skeys(k), wr=[sqk], q="act")
        P.mm(c.PS[5][0:1, 0:n], c.s_ones_col_b[:], sq, start=(k == 0), stop=(k == KT - 1),
             rd=[sqk, "c1"], wr=["PS5"])
    P.ts(mu, c.PS[4][0:1, 0:n], 1.0 / D, None, ALU.mult, rd=["PS4"], wr=["GRmu"])
    P.ts(rstd, c.PS[5][0:1, 0:n], 1.0 / D, None, ALU.mult, rd=["PS5"], wr=["GRrstd"])
    P.act(tmp, mu, AF.Square, rd=["GRmu"], wr=["ET0"])
    P.tt(rstd, rstd, tmp, ALU.subtract, rd=["ET0", "GRrstd"], wr=["GRrstd"])
    epsap, epsk = c.ccol(1e-5)
    P.act(tmp, rstd, AF.Ln, bias=epsap[0:1, :], rd=["GRrstd", epsk], wr=["ET0"])
    P.act(rstd, tmp, AF.Exp, scale=-0.5, rd=["ET0"], wr=["GRrstd"])
    P.mm(c.PS[4][:, 0:n], c.s_ones_row_f[:], mu, start=True, stop=True,
         rd=["GRmu", "c3"], wr=["PS4"])
    P.mm(c.PS[5][:, 0:n], c.s_ones_row_f[:], rstd, start=True, stop=True,
         rd=["GRrstd", "c3"], wr=["PS5"])
    for k in range(KT):
        e32 = c.ET[k % 2][:, 0:n]
        P.tt(e32, src(k), c.PS[4][:, 0:n], ALU.subtract,
             rd=skeys(k) + ["PS4"], wr=[f"ET{k%2}"])
        if gcol is None:
            P.tt(out_tiles[k], e32, c.PS[5][:, 0:n], ALU.mult,
                 rd=[f"ET{k%2}", "PS5"], wr=[out_keys[k]])
        else:
            P.tt(e32, e32, c.PS[5][:, 0:n], ALU.mult,
                 rd=[f"ET{k%2}", "PS5"], wr=[f"ET{k%2}"])
            if y_hbm_plain is not None:
                ebp = c.EB[k % 2][:, 0:n]
                P.copy(ebp, e32, rd=[f"ET{k%2}"], wr=[f"EB{k%2}"])
                P.dma("sp", y_hbm_plain[k * 128:(k + 1) * 128, t0:t0 + n], ebp,
                      rd=[f"EB{k%2}"], wr=["yhbmp"])
            P.ts(out_tiles[k], e32, gcol[:, k:k + 1], bcol_[:, k:k + 1], ALU.mult, ALU.add,
                 rd=[f"ET{k%2}"] + list(gb_keys), wr=[out_keys[k]])
    if y_hbm is not None:
        for k in range(KT):
            P.dma("sp", y_hbm[k * 128:(k + 1) * 128, t0:t0 + n], out_tiles[k], rd=[out_keys[k]], wr=["yhbm"])


def emit_phaseout2(P, c, xin, xout, scale, tag, acc_fn=None, acc_keys_fn=None):
    if acc_fn is None:
        acc_fn = lambda k, t0, n: bcol(c, k, t0, n)
        acc_keys_fn = lambda k: [f"B{k}"]
    i = 0
    for s in range(NSLC):
        t0 = s * TS
        for k in range(KT):
            e32 = c.ET[i % 2][:]
            ek = f"ET{i % 2}"
            P.dma("sp", e32, xin[k * 128:(k + 1) * 128, t0:t0 + TS], rd=["xres"], wr=[ek])
            P.stt(e32, acc_fn(k, t0, TS), scale, e32, ALU.mult, ALU.add,
                  rd=acc_keys_fn(k), wr=[ek], q="dve")
            P.dma("sp", xout[k * 128:(k + 1) * 128, t0:t0 + TS], e32, rd=[ek], wr=["xres"])
            i += 1
    P.fence("sp", ["xres"])


def emit_l1ffn(P, c, W, xin, xout, uid):
    tag = f"l1f{uid}_"
    wgT = P.input("L1_ffn_wgT", W["L1_ffn_wgT"])
    wuT = P.input("L1_ffn_wuT", W["L1_ffn_wuT"])
    wdT = P.input("L1_ffn_wdT", W["L1_ffn_wdT"])
    xsf = b_f32(c, 0, KT * TS)
    for s in range(NSLC):
        t0 = s * TS
        tiles = [acol(c, k, t0, TS) for k in range(KT)]
        keys = [f"A{k}" for k in range(KT)]
        emit_ln_slice(P, c, xin, t0, TS, xsf, ["B0", "B1"], tiles, keys, tag + f"ln{s}_")
    for k in range(KT):
        P.memset(c.B[:, k * T:(k + 1) * T], 0.0, wr=[f"B{k}"], q="pool")
    FT = 768
    NF = FT // 128
    WG = ["W1g0", "W1g1"]; WU = ["W1g1", "W1g2"]; WD = ["W2g0", "W2g1"]; HK = ["W2g1", "W2g2"]
    for f4 in range(4):
        P.dma("pool", c.W1[:, 0:KT * FT].rearrange("p (k t) -> p k t", k=KT),
              wgT[:, f4 * FT:(f4 + 1) * FT].rearrange("(k p) t -> p k t", p=128), rd=[], wr=WG)
        P.dma("pool", c.W1[:, 6144:6144 + KT * FT].rearrange("p (k t) -> p k t", k=KT),
              wuT[:, f4 * FT:(f4 + 1) * FT].rearrange("(k p) t -> p k t", p=128), rd=[], wr=WU)
        P.dma("pool", c.W2[:, 0:NF * 1024].rearrange("p (k t) -> p k t", k=NF),
              wdT[f4 * FT:(f4 + 1) * FT, :].rearrange("(k p) t -> p k t", p=128), rd=[], wr=WD)
        for s in range(NSLC):
            t0 = s * TS
            for f in range(NF):
                for k in range(KT):
                    P.mm(c.PS[0][:], c.W1[:, k * FT + f * 128: k * FT + f * 128 + 128],
                         acol(c, k, t0, TS), start=(k == 0), stop=(k == KT - 1),
                         rd=WG + [f"A{k}"], wr=["PS0"])
                hs = c.W2[:, 6144 + f * 512: 6144 + (f + 1) * 512]
                P.act(hs, c.PS[0][:], AF.Silu, scale=1.0 / 127.0, rd=["PS0"], wr=HK)
                for k in range(KT):
                    P.mm(c.PS[1][:], c.W1[:, 6144 + k * FT + f * 128: 6144 + k * FT + f * 128 + 128],
                         acol(c, k, t0, TS), start=(k == 0), stop=(k == KT - 1),
                         rd=WU + [f"A{k}"], wr=["PS1"])
                P.tt(hs, c.PS[1][:], hs, ALU.mult, rd=["PS1"] + HK, wr=HK)
            for dk in range(KT):
                pb = c.PS[2 + dk % 2]
                pk = f"PS{2 + dk % 2}"
                for f in range(NF):
                    P.mm(pb[:], c.W2[:, f * 1024 + dk * 128: f * 1024 + dk * 128 + 128],
                         c.W2[:, 6144 + f * 512: 6144 + (f + 1) * 512],
                         start=(f == 0), stop=(f == NF - 1),
                         rd=WD + HK, wr=[pk])
                P.stt(bcol(c, dk, t0, TS), pb[:], 1.0 / (127.0 * 127.0), bcol(c, dk, t0, TS),
                      ALU.mult, ALU.add, rd=[pk, f"B{dk}"], wr=[f"B{dk}"])
    emit_phaseout2(P, c, xin, xout, 0.5, tag)


def emit_l2ffn(P, c, W, xin, xout, uid):
    tag = f"l2f{uid}_"
    w1T = P.input("L2_ffn_w1T", W["L2_ffn_w1T"])
    w2T = P.input("L2_ffn_w2T", W["L2_ffn_w2T"])
    b1c = P.input("L2_ffn_b1c", W["L2_ffn_b1c"])
    b2c = P.input("L2_ffn_b2c", W["L2_ffn_b2c"])
    sb1 = c.sbuf_param(P, "L2_ffn_b1c", b1c, [128, 32])
    sb2 = c.sbuf_param(P, "L2_ffn_b2c", b2c, [128, 8])
    xsf = b_f32(c, 0, KT * TS)
    for s in range(NSLC):
        t0 = s * TS
        tiles = [acol(c, k, t0, TS) for k in range(KT)]
        keys = [f"A{k}" for k in range(KT)]
        emit_ln_slice(P, c, xin, t0, TS, xsf, ["B0", "B1"], tiles, keys, tag + f"ln{s}_")
    for k in range(KT):
        P.memset(c.B[:, k * T:(k + 1) * T], 0.0, wr=[f"B{k}"], q="pool")
    P.memset(c.GR[0:1, 0:TS], 0.0, wr=["GRmu"], q="dve")
    P.memset(c.GR[0:1, TS:2 * TS], 0.0, wr=["GRrstd"], q="dve")
    a2keys = ["GRmu", "GRrstd"]
    E8 = 512
    for e8 in range(8):
        h0 = e8 * E8
        P.dma("pool", c.W1[:, 0:4096].rearrange("p (k t) -> p k t", k=KT),
              w1T[:, h0:h0 + E8].rearrange("(k p) t -> p k t", p=128), wr=["W1g0"])
        P.dma("pool", c.W2[:, 0:4096].rearrange("p (k t) -> p k t", k=4),
              w2T[h0:h0 + E8, :].rearrange("(k p) t -> p k t", p=128), wr=["W2g0"])
        for k in range(KT):
            ab = c.EB[k % 2][:]
            P.act(ab, c.W1[:, k * 512:(k + 1) * 512], AF.Abs, rd=["W1g0"], wr=[f"EB{k%2}"])
            P.mm(c.PS[4][0:1, 0:512], c.s_ones_col_b[:], ab, start=(k == 0), stop=(k == KT - 1),
                 rd=[f"EB{k%2}", "c1"], wr=["PS4"])
        arow = c.ET[1][:].bitcast(BF16)[0:1, 0:512]
        P.ts(arow, c.PS[4][0:1, 0:512], 1.0 / D, None, ALU.mult, rd=["PS4"], wr=["ET1"])
        for j in range(4):
            P.trans(c.PTB[:, 2 * j:2 * j + 1], arow[:, j * 128:(j + 1) * 128], c.s_ones_col_b[0:1, :],
                    rd=["ET1", "c1"], wr=["PTB"])
        for j in range(4):
            P.copy(c.CMB[:, 8 + j: 9 + j], c.PTB[:, 2 * j:2 * j + 1], rd=["PTB"], wr=["CMBa1"])
        for ch in range(2):
            for k in range(4):
                ab = c.EB[k % 2][:]
                P.act(ab, c.W2[:, k * 1024 + ch * 512: k * 1024 + (ch + 1) * 512], AF.Abs,
                      rd=["W2g0"], wr=[f"EB{k%2}"])
                P.mm(c.PS[5][0:1, 0:512], c.s_ones_col_b[:], ab, start=(k == 0), stop=(k == 3),
                     rd=[f"EB{k%2}", "c1"], wr=["PS5"])
            dst = c.GR[0:1, ch * TS: ch * TS + 512]
            if e8 == 0:
                P.copy(dst, c.PS[5][0:1, 0:512], rd=["PS5"], wr=[a2keys[ch]])
            else:
                P.tt(dst, dst, c.PS[5][0:1, 0:512], ALU.add, rd=["PS5", a2keys[ch]], wr=[a2keys[ch]])
        for k in range(KT):
            P.op("act", lambda e, o=c.W1[:, 4096 + k * 512: 4096 + (k + 1) * 512],
                 i=c.W1[:, k * 512:(k + 1) * 512]: e.sign(out=o, in_=i),
                 rd=["W1g0"], wr=["W1g1"])
        for k in range(4):
            P.op("act", lambda e, o=c.W2[:, 4096 + k * 1024: 4096 + (k + 1) * 1024],
                 i=c.W2[:, k * 1024:(k + 1) * 1024]: e.sign(out=o, in_=i),
                 rd=["W2g0"], wr=["W2g1"])
        for s in range(NSLC):
            t0 = s * TS
            for f in range(4):
                for k in range(KT):
                    P.mm(c.PS[0][:], c.W1[:, 4096 + k * 512 + f * 128: 4096 + k * 512 + f * 128 + 128],
                         acol(c, k, t0, TS), start=(k == 0), stop=(k == KT - 1),
                         rd=["W1g1", f"A{k}"], wr=["PS0"])
                e32 = c.ET[0][:]
                P.ts(e32, c.PS[0][:], c.CMB[:, 8 + f:9 + f], None, ALU.mult,
                     rd=["PS0", "CMBa1"], wr=["ET0"])
                hidx = (h0 + f * 128) // 128
                hs = c.W1[:, 8192 + f * 512: 8192 + (f + 1) * 512]
                P.act(hs, e32, AF.Gelu, bias=sb1[:, hidx:hidx + 1],
                      rd=["ET0", "pL2_ffn_b1c"], wr=["W1g2"])
            for dk in range(KT):
                pb = c.PS[2 + dk % 2]
                pk = f"PS{2 + dk % 2}"
                for f in range(4):
                    P.mm(pb[:], c.W2[:, 4096 + f * 1024 + dk * 128: 4096 + f * 1024 + dk * 128 + 128],
                         c.W1[:, 8192 + f * 512: 8192 + (f + 1) * 512],
                         start=(f == 0), stop=(f == 3),
                         rd=["W2g1", "W1g2"], wr=[pk])
                P.tt(bcol(c, dk, t0, TS), pb[:], bcol(c, dk, t0, TS), ALU.add,
                     rd=[pk, f"B{dk}"], wr=[f"B{dk}"])
    for ch in range(2):
        arow = c.ET[1][:].bitcast(BF16)[0:1, 0:512]
        P.ts(arow, c.GR[0:1, ch * TS: ch * TS + 512], 1.0 / 4096.0, None, ALU.mult,
             rd=[a2keys[ch]], wr=["ET1"])
        for j in range(4):
            P.trans(c.PTB[:, 2 * j:2 * j + 1], arow[:, j * 128:(j + 1) * 128], c.s_ones_col_b[0:1, :],
                    rd=["ET1", "c1"], wr=["PTB"])
        for j in range(4):
            P.copy(c.CMB[:, 16 + ch * 4 + j: 17 + ch * 4 + j], c.PTB[:, 2 * j:2 * j + 1], rd=["PTB"], wr=["CMBa2"])
    i = 0
    for s in range(NSLC):
        t0 = s * TS
        for k in range(KT):
            e32 = c.ET[0][:]
            eb = c.EB[i % 2][:]
            P.ts(eb, bcol(c, k, t0, TS), c.CMB[:, 16 + k:17 + k], sb2[:, k:k + 1], ALU.mult, ALU.add,
                 rd=[f"B{k}", "CMBa2", "pL2_ffn_b2c"], wr=[f"EB{i%2}"])
            P.dma("sp", e32, xin[k * 128:(k + 1) * 128, t0:t0 + TS], rd=["xres"], wr=["ET0"])
            P.stt(e32, eb, 0.5, e32, ALU.mult, ALU.add, rd=[f"EB{i%2}"], wr=["ET0"], q="dve")
            P.dma("sp", xout[k * 128:(k + 1) * 128, t0:t0 + TS], e32, rd=["ET0"], wr=["xres"])
            i += 1
    P.fence("sp", ["xres"])


def emit_l0attn(P, c, W, xin, xout, uid):
    tag = f"l0a{uid}_"
    Wa = P.input("L0_attn_Wa", W["L0_attn_Wa"])
    Wb = P.input("L0_attn_Wb", W["L0_attn_Wb"])
    Wc = P.input("L0_attn_Wc", W["L0_attn_Wc"])
    W1 = P.input("L0_attn_W1", W["L0_attn_W1"])
    W2 = P.input("L0_attn_W2", W["L0_attn_W2"])
    pv = c.sbuf_param(P, "L0_attn_vec", P.input("L0_attn_vec", W["L0_attn_vec"]), [128, 64])
    PK = ["pL0_attn_vec"]
    bm1 = c.sbuf_param(P, "L0_attn_bm1", P.input("L0_attn_bm1", W["L0_attn_bm1"]), [128, 32])
    wreg = lambda k, w: c.W1[:, k * 1024 + w * 128: k * 1024 + w * 128 + 128]
    WR = ["W1g0", "W1g1"]
    def a_t(k):
        if k < 4:
            return c.W1[:].bitcast(F32)[:, 4096 + k * 512: 4096 + (k + 1) * 512]
        return c.W2[:].bitcast(F32)[:, (k - 4) * 512: (k - 4 + 1) * 512]
    AK = lambda k: ["W1g2"] if k < 4 else ["W2g0"]
    def u_t(k):
        return c.W2[:].bitcast(F32)[:, 2048 + k * 512: 2048 + (k + 1) * 512]
    UK = lambda k: ["W2g1", "W2g2"]
    CHK = TS
    for ch in range(NSLC):
        t0 = ch * TS
        xsf = b_f32(c, 0, KT * TS)
        ytiles = [c.B[:, 2 * T + 4096 * 0 + k * 512: 2 * T + k * 512 + 512] for k in range(KT)]
        ykeys = ["B2"] * KT
        emit_ln_slice(P, c, xin, t0, TS, xsf, ["B0", "B1"], ytiles, ykeys, tag + f"ln{ch}_")
        htiles = [c.B[:, 3 * T + k * 512: 3 * T + k * 512 + 512] for k in range(KT)]
        hkeys = ["B3"] * KT
        emit_ln_slice(P, c, None, t0, TS, None, None, htiles, hkeys, tag + f"lnh{ch}_",
                      gcol=pv[:, 0:8], bcol_=pv[:, 8:16], gb_keys=PK,
                      src_tiles=ytiles, src_keys=ykeys)
        P.dma("pool", c.W1[:, 0:8192].rearrange("p (k t) -> p k t", k=KT),
              Wa.rearrange("(k p) t -> p k t", p=128), wr=WR)
        for f in range(KT):
            for k in range(KT):
                P.mm(c.PS[0][:], wreg(k, f), htiles[k], start=(k == 0), stop=(k == KT - 1),
                     rd=WR + [hkeys[k]], wr=["PS0"])
            P.act(a_t(f), c.PS[0][:], AF.Sigmoid, bias=pv[:, 16 + f:17 + f],
                  rd=["PS0"] + PK, wr=AK(f))
        P.dma("pool", c.W1[:, 0:8192].rearrange("p (k t) -> p k t", k=KT),
              Wb.rearrange("(k p) t -> p k t", p=128), wr=WR)
        for f in range(KT):
            for k in range(KT):
                P.mm(c.PS[0][:], wreg(k, f), htiles[k], start=(k == 0), stop=(k == KT - 1),
                     rd=WR + [hkeys[k]], wr=["PS0"])
            P.act(c.ET[0][:], c.PS[0][:], AF.Tanh, bias=pv[:, 24 + f:25 + f],
                  rd=["PS0"] + PK, wr=["ET0"])
            P.ts(c.ET[1][:], a_t(f), -1.0, 1.0, ALU.mult, ALU.add, rd=AK(f), wr=["ET1"])
            P.tt(u_t(f), c.ET[1][:], c.ET[0][:], ALU.mult, rd=["ET0", "ET1"], wr=UK(f))
        for f in range(KT):
            init = 0.0 if ch == 0 else c.CARR[:, f:f + 1]
            rdl = AK(f) + UK(f) + (["CARR"] if ch else [])
            P.scan(u_t(f), a_t(f), u_t(f), init, rd=rdl, wr=UK(f))
            P.copy(c.CARR[:, f:f + 1], u_t(f)[:, TS - 1:TS], rd=UK(f), wr=["CARR"])
        P.dma("pool", c.W1[:, 0:8192].rearrange("p (k t) -> p k t", k=KT),
              Wc.rearrange("(k p) t -> p k t", p=128), wr=WR)
        for f in range(KT):
            for k in range(KT):
                P.mm(c.PS[0][:], wreg(k, f), htiles[k], start=(k == 0), stop=(k == KT - 1),
                     rd=WR + [hkeys[k]], wr=["PS0"])
            P.act(c.ET[0][:], c.PS[0][:], AF.Sigmoid, bias=pv[:, 32 + f:33 + f],
                  rd=["PS0"] + PK, wr=["ET0"])
            P.tt(c.ET[1][:], c.ET[0][:], u_t(f), ALU.mult, rd=["ET0"] + UK(f), wr=["ET1"])
            P.tt(acol(c, f, t0, TS), c.ET[1][:], ytiles[f], ALU.add,
                 rd=["ET1", ykeys[f]], wr=[f"A{f}"])
    for k in range(KT):
        P.memset(c.B[:, k * T:(k + 1) * T], 0.0, wr=[f"B{k}"], q="pool")
    FC = 1024
    for fc in range(4):
        P.dma("pool", c.W1[:, 0:8192].rearrange("p (k t) -> p k t", k=KT),
              W1[:, fc * FC:(fc + 1) * FC].rearrange("(k p) t -> p k t", p=128), wr=WR)
        P.dma("pool", c.W2[:, 0:8192].rearrange("p (k t) -> p k t", k=KT),
              W2[fc * FC:(fc + 1) * FC, :].rearrange("(k p) t -> p k t", p=128), wr=["W2g0", "W2g1"])
        for s in range(NSLC):
            t0 = s * TS
            h2tiles = [c.W2[:, 8192 + k * 512: 8192 + (k + 1) * 512] for k in range(KT)]
            h2keys = ["W2g2"] * KT
            ztiles = [acol(c, k, t0, TS) for k in range(KT)]
            zkeys = [f"A{k}" for k in range(KT)]
            emit_ln_slice(P, c, None, t0, TS, None, None, h2tiles, h2keys, tag + f"lnz{fc}_{s}_",
                          gcol=pv[:, 40:48], bcol_=pv[:, 48:56], gb_keys=PK,
                          src_tiles=ztiles, src_keys=zkeys)
            for f in range(KT):
                for k in range(KT):
                    P.mm(c.PS[0][:], wreg(k, f), h2tiles[k], start=(k == 0), stop=(k == KT - 1),
                         rd=WR + h2keys, wr=["PS0"])
                midx = fc * 8 + f
                P.act(c.W1[:, 8192 + f * 512: 8192 + (f + 1) * 512], c.PS[0][:],
                      AF.Gelu, bias=bm1[:, midx:midx + 1],
                      rd=["PS0", "pL0_attn_bm1"], wr=["W1g2"])
            for dk in range(KT):
                pb = c.PS[2 + dk % 2]
                pk = f"PS{2 + dk % 2}"
                for f in range(KT):
                    P.mm(pb[:], c.W2[:, f * 1024 + dk * 128: f * 1024 + dk * 128 + 128],
                         c.W1[:, 8192 + f * 512: 8192 + (f + 1) * 512],
                         start=(f == 0), stop=(f == KT - 1),
                         rd=["W2g0", "W2g1", "W1g2"], wr=[pk])
                P.tt(bcol(c, dk, t0, TS), pb[:], bcol(c, dk, t0, TS), ALU.add,
                     rd=[pk, f"B{dk}"], wr=[f"B{dk}"])
    i = 0
    for s in range(NSLC):
        t0 = s * TS
        for k in range(KT):
            e32 = c.ET[0][:]
            P.dma("sp", e32, xin[k * 128:(k + 1) * 128, t0:t0 + TS], rd=["xres"], wr=["ET0"])
            et1 = c.ET[1][:]
            P.tt(et1, acol(c, k, t0, TS), bcol(c, k, t0, TS), ALU.add,
                 rd=[f"A{k}", f"B{k}"], wr=["ET1"])
            P.ts(et1, et1, pv[:, 56 + k:57 + k], None, ALU.add, rd=["ET1"] + PK, wr=["ET1"])
            P.tt(e32, e32, et1, ALU.add, rd=["ET1"], wr=["ET0"])
            P.dma("sp", xout[k * 128:(k + 1) * 128, t0:t0 + TS], e32, rd=["ET0"], wr=["xres"])
            i += 1
    P.fence("sp", ["xres"])


def emit_l1attn(P, c, W, xin, xout, uid):
    tag = f"l1a{uid}_"
    WqT = P.input("L1_attn_WqT", W["L1_attn_WqT"])
    WkT = P.input("L1_attn_WkT", W["L1_attn_WkT"])
    WvT = P.input("L1_attn_WvT", W["L1_attn_WvT"])
    WoT = P.input("L1_attn_WoT", W["L1_attn_WoT"])
    pv = c.sbuf_param(P, "L1_attn_vec", P.input("L1_attn_vec", W["L1_attn_vec"]), [128, 32])
    PK = ["pL1_attn_vec"]
    b3f = c.B[:].bitcast(F32)
    bkrow = b3f[0:1, 3 * 2048: 3 * 2048 + 1024]
    bvrow = b3f[0:1, 3 * 2048 + 1024: 3 * 2048 + 2048]
    for j in range(KT):
        P.copy(c.EB[0][:, 0:1], pv[:, 8 + j:9 + j], rd=PK, wr=["EB0"])
        P.trans(c.PTB[0:1, 0:128], c.EB[0][:, 0:1], c.s_idb[:], rd=["EB0", "c4"], wr=["PTB"])
        P.copy(bkrow[:, j * 128:(j + 1) * 128], c.PTB[0:1, 0:128], rd=["PTB"], wr=["B3"])
        P.copy(c.EB[0][:, 0:1], pv[:, 16 + j:17 + j], rd=PK, wr=["EB0"])
        P.trans(c.PTB[0:1, 0:128], c.EB[0][:, 0:1], c.s_idb[:], rd=["EB0", "c4"], wr=["PTB"])
        P.copy(bvrow[:, j * 128:(j + 1) * 128], c.PTB[0:1, 0:128], rd=["PTB"], wr=["B3"])
    P.dma("pool", c.W1[:, 0:8192].rearrange("p (k t) -> p k t", k=KT),
          WqT.rearrange("(k p) t -> p k t", p=128), wr=["W1g0", "W1g1"])
    P.dma("pool", c.W1[:, 8192:12288].rearrange("p (k t) -> p k t", k=4),
          WkT[0:512, :].rearrange("(k p) t -> p k t", p=128), wr=["W1g2"])
    P.dma("pool", c.W2[:, 0:4096].rearrange("p (k t) -> p k t", k=4),
          WkT[512:1024, :].rearrange("(k p) t -> p k t", p=128), wr=["W2g0"])
    P.dma("pool", c.W2[:, 4096:12288].rearrange("p (k t) -> p k t", k=KT),
          WvT.rearrange("(k p) t -> p k t", p=128), wr=["W2g1", "W2g2"])

    def wq(k, f):
        return c.W1[:, k * 1024 + f * 128: k * 1024 + f * 128 + 128]

    def wk(k):
        if k < 4:
            return c.W1[:, 8192 + k * 1024: 8192 + (k + 1) * 1024]
        return c.W2[:, (k - 4) * 1024: (k - 4 + 1) * 1024]
    WKK = ["W1g2", "W2g0"]

    def wv(k):
        return c.W2[:, 4096 + k * 1024: 4096 + (k + 1) * 1024]
    WVK = ["W2g1", "W2g2"]
    b5f = c.B[:].bitcast(F32)
    kv_sb = b5f[:, 5 * 2048: 5 * 2048 + 512]
    P.memset(kv_sb, 0.0, wr=["B5"], q="dve")
    xsf = b_f32(c, 0, KT * TS)
    for s in range(NSLC):
        t0 = s * TS
        ytiles = [c.B[:, 2 * T + k * 512: 2 * T + k * 512 + 512] for k in range(KT)]
        ykeys = ["B2"] * KT
        emit_ln_slice(P, c, xin, t0, TS, xsf, ["B0", "B1"], ytiles, ykeys, tag + f"ln{s}_")
        for f in range(KT):
            for k in range(KT):
                P.mm(c.PS[0][:], wq(k, f), ytiles[k], start=(k == 0), stop=(k == KT - 1),
                     rd=["W1g0", "W1g1", ykeys[k]], wr=["PS0"])
            P.ts(c.ET[0][:], c.PS[0][:], pv[:, f:f + 1], 0.0, ALU.add, ALU.min,
                 rd=["PS0"] + PK, wr=["ET0"])
            P.act(c.ET[0][:], c.ET[0][:], AF.Exp, rd=["ET0"], wr=["ET0"])
            P.ts(c.ET[1][:], c.PS[0][:], pv[:, f:f + 1], 0.0, ALU.add, ALU.max,
                 rd=["PS0"] + PK, wr=["ET1"])
            P.tt(acol(c, f, t0, TS), c.ET[0][:], c.ET[1][:], ALU.add,
                 rd=["ET0", "ET1"], wr=[f"A{f}"])
        b4 = c.B[:, 4 * T: 4 * T + 4096]
        for tt in range(4):
            ts0 = t0 + tt * 128
            k_tm = b4[:, 0:1024]
            v_tm = b4[:, 1024:2048]
            for half in range(2):
                ps = c.PS[half]
                pk = f"PS{half}"
                for k in range(KT):
                    P.mm(ps[:], ytiles[k][:, tt * 128:(tt + 1) * 128], wk(k)[:, half * 512:(half + 1) * 512],
                         start=(k == 0), stop=False, rd=[ykeys[k]] + WKK, wr=[pk])
                P.mm(ps[:], c.s_ones_row_f[:], bkrow[:, half * 512:(half + 1) * 512],
                     start=False, stop=True, rd=["B3", "c3"], wr=[pk])
                P.ts(c.ET[0][:], ps[:], 0.0, None, ALU.min, rd=[pk], wr=["ET0"])
                P.act(c.ET[0][:], c.ET[0][:], AF.Exp, rd=["ET0"], wr=["ET0"])
                P.ts(c.ET[1][:], ps[:], 0.0, None, ALU.max, rd=[pk], wr=["ET1"])
                P.tt(k_tm[:, half * 512:(half + 1) * 512], c.ET[0][:], c.ET[1][:], ALU.add,
                     rd=["ET0", "ET1"], wr=["B4"])
            for half in range(2):
                ps = c.PS[2 + half]
                pk = f"PS{2 + half}"
                for k in range(KT):
                    P.mm(ps[:], ytiles[k][:, tt * 128:(tt + 1) * 128], wv(k)[:, half * 512:(half + 1) * 512],
                         start=(k == 0), stop=False, rd=[ykeys[k]] + WVK, wr=[pk])
                P.mm(ps[:], c.s_ones_row_f[:], bvrow[:, half * 512:(half + 1) * 512],
                     start=False, stop=True, rd=["B3", "c3"], wr=[pk])
                P.copy(v_tm[:, half * 512:(half + 1) * 512], ps[:], rd=[pk], wr=["B4"])
            for h in range(16):
                P.mm(c.PS[4][(h % 2) * 64:(h % 2) * 64 + 64, (h // 2) * 64:(h // 2 + 1) * 64],
                     k_tm[:, h * 64:(h + 1) * 64], v_tm[:, h * 64:(h + 1) * 64],
                     start=True, stop=True, rd=["B4"], wr=["PS4"])
            P.tt(kv_sb, kv_sb, c.PS[4][:], ALU.add, rd=["PS4", "B5"], wr=["B5"])
    kv_b = c.B[:, 5 * T + 2048: 5 * T + 2048 + 512]
    P.copy(kv_b, kv_sb, rd=["B5"], wr=["B5b"])
    P.dma("pool", c.W1[:, 0:8192].rearrange("p (k t) -> p k t", k=KT),
          WoT.rearrange("(k p) t -> p k t", p=128), wr=["W1g0", "W1g1"])
    for s in range(NSLC):
        t0 = s * TS
        ao = c.B[:, 6 * T: 6 * T + 4096]
        for hp in range(KT):
            for j in range(2):
                h = hp * 2 + j
                qh = acol(c, h // 2, t0, TS)[(h % 2) * 64:(h % 2) * 64 + 64, :]
                kvh = kv_b[(h % 2) * 64:(h % 2) * 64 + 64, (h // 2) * 64:(h // 2 + 1) * 64]
                P.mm(c.PS[0][j * 64:(j + 1) * 64, :], kvh, qh,
                     start=True, stop=True, rd=["B5b", f"A{h // 2}"], wr=["PS0"])
            P.copy(ao[:, hp * 512:(hp + 1) * 512], c.PS[0][:], rd=["PS0"], wr=["B6"])
        for dk in range(KT):
            pb = c.PS[2 + dk % 2]
            pk = f"PS{2 + dk % 2}"
            for f in range(KT):
                P.mm(pb[:], c.W1[:, f * 1024 + dk * 128: f * 1024 + dk * 128 + 128],
                     ao[:, f * 512:(f + 1) * 512], start=(f == 0), stop=(f == KT - 1),
                     rd=["W1g0", "W1g1", "B6"], wr=[pk])
            e32 = c.ET[0][:]
            P.ts(e32, pb[:], pv[:, 24 + dk:25 + dk], None, ALU.add, rd=[pk] + PK, wr=["ET0"])
            et1 = c.ET[1][:]
            P.dma("sp", et1, xin[dk * 128:(dk + 1) * 128, t0:t0 + TS], rd=["xres"], wr=["ET1"])
            P.tt(e32, e32, et1, ALU.add, rd=["ET1"], wr=["ET0"])
            P.dma("sp", xout[dk * 128:(dk + 1) * 128, t0:t0 + TS], e32, rd=["ET0"], wr=["xres"])
    P.fence("sp", ["xres"])


def emit_routing(P, c, W, ytiles, ykeys, s, tag, ne, wr_ap, wrkeys, br_row, brkeys,
                 sigmoid=False, topk=1, ebias_tile=None, ebk=(), route_scale=1.0):
    t0 = s * TS
    for tt in range(4):
        i = s * 4 + tt
        for k in range(KT):
            P.mm(c.PS[1][:, 0:ne], ytiles[k][:, tt * 128:(tt + 1) * 128], wr_ap(k),
                 start=(k == 0), stop=(br_row is None and k == KT - 1), rd=[ykeys[k]] + wrkeys,
                 wr=["PS1"])
        if br_row is not None:
            P.mm(c.PS[1][:, 0:ne], c.s_ones_row_f[:], br_row, start=False, stop=True,
                 rd=brkeys + ["c3"], wr=["PS1"])
        lg = c.EB[0][:, 0:ne].bitcast(F32)[:, 0:ne] if False else c.ET[1][:, 0:ne]
        if sigmoid:
            P.act(lg, c.PS[1][:, 0:ne], AF.Sigmoid, rd=["PS1"], wr=["ET1"])
        else:
            P.copy(lg, c.PS[1][:, 0:ne], rd=["PS1"], wr=["ET1"])
        sel = c.ET[1][:, 64:64 + ne]
        if ebias_tile is not None:
            P.tt(sel, lg, ebias_tile, ALU.add, rd=["ET1"] + list(ebk), wr=["ET1"])
        else:
            P.copy(sel, lg, rd=["ET1"], wr=["ET1"])
        m = c.ET[1][:, 32:33]
        if topk == 1:
            P.red(m, sel, ALU.max, rd=["ET1"], wr=["ET1"])
        else:
            work = c.ET[1][:, 128:128 + ne]
            P.copy(work, sel, rd=["ET1"], wr=["ET1"])
            for it in range(topk):
                P.red(m, work, ALU.max, rd=["ET1"], wr=["ET1"])
                if it < topk - 1:
                    dd = c.ET[1][:, 96:96 + ne]
                    P.ts(dd, work, m, None, ALU.subtract, rd=["ET1"], wr=["ET1"])
                    P.ts(dd, dd, 0.0, None, ALU.is_ge, rd=["ET1"], wr=["ET1"])
                    P.stt(work, dd, -1e30, work, ALU.mult, ALU.add, rd=["ET1"], wr=["ET1"])
        dd = c.ET[1][:, 96:96 + ne]
        P.ts(dd, sel, m, None, ALU.subtract, rd=["ET1"], wr=["ET1"])
        P.ts(dd, dd, 0.0, None, ALU.is_ge, rd=["ET1"], wr=["ET1"])
        gcol = c.ET[1][:, 33:34]
        if topk == 1:
            ex = c.ET[1][:, 128:128 + ne]
            P.ts(ex, sel, m, None, ALU.subtract, rd=["ET1"], wr=["ET1"])
            P.act(ex, ex, AF.Exp, rd=["ET1"], wr=["ET1"])
            P.red(gcol, ex, ALU.add, rd=["ET1"], wr=["ET1"])
            P.recip(gcol, gcol, rd=["ET1"], wr=["ET1"])
            P.ts(c.CMB[:, i * ne:(i + 1) * ne], dd, gcol, None, ALU.mult,
                 rd=["ET1"], wr=["CMB"])
        else:
            cr = c.ET[1][:, 128:128 + ne]
            P.tt(cr, dd, lg, ALU.mult, rd=["ET1"], wr=["ET1"])
            P.red(gcol, cr, ALU.add, rd=["ET1"], wr=["ET1"])
            P.recip(gcol, gcol, rd=["ET1"], wr=["ET1"])
            P.ts(gcol, gcol, route_scale, None, ALU.mult, rd=["ET1"], wr=["ET1"])
            P.ts(c.CMB[:, i * ne:(i + 1) * ne], cr, gcol, None, ALU.mult,
                 rd=["ET1"], wr=["CMB"])


def emit_comb_cm(P, c, ne, tag):
    af = c.A[:].bitcast(F32)
    for i in range(32):
        eb = c.EB[0][:, 0:ne]
        P.copy(eb, c.CMB[:, i * ne:(i + 1) * ne], rd=["CMB"], wr=["EB0"])
        P.trans(c.PTB[0:ne, 0:128], eb, c.s_idb[:], rd=["EB0", "c4"], wr=["PTB"])
        P.copy(af[0:ne, i * 128:(i + 1) * 128], c.PTB[0:ne, 0:128], rd=["PTB"], wr=["A0", "A1"])
    return af[0:ne, :]


def emit_l0ffn(P, c, W, xin, xout, uid):
    tag = f"l0f{uid}_"
    W1 = [P.input(f"L0_ffn_W1_{e}", W["L0_ffn_W1"][e]) for e in range(8)]
    W2 = [P.input(f"L0_ffn_W2_{e}", W["L0_ffn_W2"][e]) for e in range(8)]
    Wr = P.input("L0_ffn_WrT", W["L0_ffn_WrT"])
    br = c.sbuf_param(P, "L0_ffn_br", P.input("L0_ffn_br", W["L0_ffn_br_row"]), [1, 8])
    b1 = c.sbuf_param(P, "L0_ffn_b1", P.input("L0_ffn_b1", W["L0_ffn_b1c"]), [128, 256])
    b2_d = P.input("L0_ffn_b2", W["L0_ffn_b2n"])
    b2 = c.A[:].bitcast(F32)[0:8, 6144:7168]
    P.dma("sp", b2, b2_d, wr=["Ab2"])
    wrs = c.sbuf_param(P, "L0_ffn_Wr", Wr, [128, 64], dtype=BF16)
    ybuf = P.nc.dram_tensor(f"yspill{uid}", [D, T], BF16, kind="Internal").ap()
    xsf = b_f32(c, 0, KT * TS)
    NE = 8
    for s in range(NSLC):
        t0 = s * TS
        ytiles = [c.B[:, 2 * T + k * 512: 2 * T + k * 512 + 512] for k in range(KT)]
        ykeys = ["B2"] * KT
        emit_ln_slice(P, c, xin, t0, TS, xsf, ["B0", "B1"], ytiles, ykeys, tag + f"ln{s}_",
                      y_hbm=ybuf)
        emit_routing(P, c, W, ytiles, ykeys, s, tag, NE,
                     lambda k: wrs[:, k * 8:(k + 1) * 8], ["pL0_ffn_Wr"],
                     br, ["pL0_ffn_br"])
    comb_cm = emit_comb_cm(P, c, NE, tag)
    for k in range(KT):
        P.memset(c.B[:, k * T:(k + 1) * T], 0.0, wr=[f"B{k}"], q="pool")
    af = c.A[:].bitcast(F32)
    yb = c.A[:, 2 * T: 2 * T + 4096]
    FC = 1024
    for e in range(NE):
        for ch in range(4):
            P.dma("pool", c.W1[:, 0:8192].rearrange("p (k t) -> p k t", k=KT),
                  W1[e][:, ch * FC:(ch + 1) * FC].rearrange("(k p) t -> p k t", p=128),
                  wr=["W1g0", "W1g1"])
            P.dma("pool", c.W2[:, 0:8192].rearrange("p (k t) -> p k t", k=KT),
                  W2[e][ch * FC:(ch + 1) * FC, :].rearrange("(k p) t -> p k t", p=128),
                  wr=["W2g0", "W2g1"])
            for s in range(NSLC):
                t0 = s * TS
                P.dma("sp", yb.rearrange("p (k t) -> p k t", k=KT),
                      ybuf[:, t0:t0 + TS].rearrange("(k p) t -> p k t", p=128),
                      rd=["yhbm"], wr=["A2"])
                for tt in range(4):
                    i = s * 4 + tt
                    P.copy(c.EB[0][:, 0:1], c.CMB[:, i * NE + e: i * NE + e + 1],
                           rd=["CMB"], wr=["EB0"])
                    P.trans(c.PTB[0:1, 0:128], c.EB[0][:, 0:1], c.s_idb[:],
                            rd=["EB0", "c4"], wr=["PTB"])
                    P.copy(c.GR[0:1, tt * 128:(tt + 1) * 128], c.PTB[0:1, 0:128],
                           rd=["PTB"], wr=["GRmu"])
                P.mm(c.PS[4][:, 0:TS], c.s_ones_row_f[:], c.GR[0:1, 0:TS],
                     start=True, stop=True, rd=["GRmu", "c3"], wr=["PS4"])
                P.copy(c.ET[1][:], c.PS[4][:, 0:TS], rd=["PS4"], wr=["ET1"])
                for f in range(KT):
                    for k in range(KT):
                        P.mm(c.PS[0][:], c.W1[:, k * FC + f * 128: k * FC + f * 128 + 128],
                             yb[:, k * 512:(k + 1) * 512], start=(k == 0), stop=(k == KT - 1),
                             rd=["W1g0", "W1g1", "A2"], wr=["PS0"])
                    bidx = e * 32 + ch * 8 + f
                    P.act(c.W1[:, 8192 + f * 512: 8192 + (f + 1) * 512], c.PS[0][:],
                          AF.Gelu, bias=b1[:, bidx:bidx + 1],
                          rd=["PS0", "pL0_ffn_b1"], wr=["W1g2"])
                for dk in range(KT):
                    pb = c.PS[2 + dk % 2]
                    pk = f"PS{2 + dk % 2}"
                    for f in range(KT):
                        P.mm(pb[:], c.W2[:, f * FC + dk * 128: f * FC + dk * 128 + 128],
                             c.W1[:, 8192 + f * 512: 8192 + (f + 1) * 512],
                             start=(f == 0), stop=(f == KT - 1),
                             rd=["W2g0", "W2g1", "W1g2"], wr=[pk])
                    eb = c.EB[dk % 2][:]
                    P.tt(eb, pb[:], c.ET[1][:], ALU.mult, rd=[pk, "ET1"], wr=[f"EB{dk%2}"])
                    P.tt(bcol(c, dk, t0, TS), bcol(c, dk, t0, TS), eb, ALU.add,
                         rd=[f"EB{dk%2}"], wr=[f"B{dk}"], q="pool")
    for s in range(NSLC):
        t0 = s * TS
        for dk in range(KT):
            P.mm(c.PS[4][:, 0:TS], b2[:, dk * 128:(dk + 1) * 128],
                 comb_cm[:, t0:t0 + TS], start=True, stop=True,
                 rd=["Ab2", "A0", "A1"], wr=["PS4"])
            e32 = c.ET[0][:]
            P.tt(e32, c.PS[4][:, 0:TS], bcol(c, dk, t0, TS), ALU.add,
                 rd=["PS4", f"B{dk}"], wr=["ET0"])
            et1 = c.ET[1][:]
            P.dma("sp", et1, xin[dk * 128:(dk + 1) * 128, t0:t0 + TS], rd=["xres"], wr=["ET1"])
            P.stt(et1, e32, 0.5, et1, ALU.mult, ALU.add, rd=["ET0"], wr=["ET1"], q="dve")
            P.dma("sp", xout[dk * 128:(dk + 1) * 128, t0:t0 + TS], et1,
                  rd=["ET1"], wr=["xres"])
    P.fence("sp", ["xres"])


def emit_l3ffn(P, c, W, xin, xout, uid):
    tag = f"l3f{uid}_"
    WG = [P.input(f"L3_ffn_wgT_{e}", W["L3_ffn_wgT"][e]) for e in range(16)]
    WU = [P.input(f"L3_ffn_wuT_{e}", W["L3_ffn_wuT"][e]) for e in range(16)]
    WD = [P.input(f"L3_ffn_wdT_{e}", W["L3_ffn_wdT"][e]) for e in range(16)]
    SG = P.input("L3_ffn_sgT", W["L3_ffn_sgT"])
    SU = P.input("L3_ffn_suT", W["L3_ffn_suT"])
    SD = P.input("L3_ffn_sdT", W["L3_ffn_sdT"])
    gw = c.sbuf_param(P, "L3_ffn_gw", P.input("L3_ffn_gwT", W["L3_ffn_gwT"]), [128, 128], dtype=BF16)
    ebr = c.sbuf_param(P, "L3_ffn_ebias", P.input("L3_ffn_ebias", W["L3_ffn_ebias_row"]), [1, 16])
    ebt = c.EBI
    P.mm(c.PS[4][:, 0:16], c.s_ones_row_f[:], ebr, start=True, stop=True,
         rd=["pL3_ffn_ebias", "c3"], wr=["PS4"])
    P.copy(ebt[:], c.PS[4][:, 0:16], rd=["PS4"], wr=["EBI"])
    ybuf = P.nc.dram_tensor(f"yspill{uid}", [D, T], BF16, kind="Internal").ap()
    xsf = b_f32(c, 0, KT * TS)
    NE = 16
    for s in range(NSLC):
        t0 = s * TS
        ytiles = [c.B[:, 2 * T + k * 512: 2 * T + k * 512 + 512] for k in range(KT)]
        ykeys = ["B2"] * KT
        emit_ln_slice(P, c, xin, t0, TS, xsf, ["B0", "B1"], ytiles, ykeys, tag + f"ln{s}_",
                      y_hbm=ybuf)
        emit_routing(P, c, W, ytiles, ykeys, s, tag, NE,
                     lambda k: gw[:, k * 16:(k + 1) * 16], ["pL3_ffn_gw"],
                     None, [], sigmoid=True, topk=8, ebias_tile=ebt[:], ebk=["EBI"],
                     route_scale=2.826)
    comb_cm = emit_comb_cm(P, c, NE, tag)
    af = c.A[:].bitcast(F32)
    yb = c.A[:, 2 * T: 2 * T + 4096]
    hb = c.A[:, 3 * T: 3 * T + 4096]
    TH = 2048
    for th in range(2):
        for k in range(KT):
            P.memset(c.B[:, k * T:(k + 1) * T].bitcast(F32), 0.0, wr=[f"B{k}"], q="pool")
        for e in range(NE + 1):
            shared = (e == NE)
            if shared:
                P.dma("pool", c.W1[:, 0:8192].rearrange("p (k t) -> p k t", k=KT),
                      SG.rearrange("(k p) t -> p k t", p=128), wr=["W1g0", "W1g1"])
                P.dma("pool", c.W1[:, 8192:12288].rearrange("p (k t) -> p k t", k=4),
                      SU[0:512, :].rearrange("(k p) t -> p k t", p=128), wr=["W1g2"])
                P.dma("pool", c.W2[:, 8192:12288].rearrange("p (k t) -> p k t", k=4),
                      SU[512:1024, :].rearrange("(k p) t -> p k t", p=128), wr=["W2g2"])
                P.dma("pool", c.W2[:, 0:8192].rearrange("p (k t) -> p k t", k=KT),
                      SD.rearrange("(k p) t -> p k t", p=128), wr=["W2g0", "W2g1"])
            else:
                P.dma("pool", c.W1[:, 0:8192].rearrange("p (k t) -> p k t", k=KT),
                      WG[e].rearrange("(k p) t -> p k t", p=128), wr=["W1g0", "W1g1"])
                P.dma("pool", c.W1[:, 8192:12288].rearrange("p (k t) -> p k t", k=4),
                      WU[e][0:512, :].rearrange("(k p) t -> p k t", p=128), wr=["W1g2"])
                P.dma("pool", c.W2[:, 8192:12288].rearrange("p (k t) -> p k t", k=4),
                      WU[e][512:1024, :].rearrange("(k p) t -> p k t", p=128), wr=["W2g2"])
                P.dma("pool", c.W2[:, 0:8192].rearrange("p (k t) -> p k t", k=KT),
                      WD[e].rearrange("(k p) t -> p k t", p=128), wr=["W2g0", "W2g1"])

            def wu_t(k):
                if k < 4:
                    return c.W1[:, 8192 + k * 1024: 8192 + (k + 1) * 1024]
                return c.W2[:, 8192 + (k - 4) * 1024: 8192 + (k - 4 + 1) * 1024]
            WUK = ["W1g2", "W2g2"]
            for s in range(4):
                t0 = th * TH + s * TS
                P.dma("sp", yb.rearrange("p (k t) -> p k t", k=KT),
                      ybuf[:, t0:t0 + TS].rearrange("(k p) t -> p k t", p=128),
                      rd=["yhbm"], wr=["A2"])
                if not shared:
                    for tt in range(4):
                        i = t0 // 128 + tt
                        P.copy(c.EB[0][:, 0:1], c.CMB[:, i * NE + e: i * NE + e + 1],
                               rd=["CMB"], wr=["EB0"])
                        P.trans(c.PTB[0:1, 0:128], c.EB[0][:, 0:1], c.s_idb[:],
                                rd=["EB0", "c4"], wr=["PTB"])
                        P.copy(c.GR[0:1, tt * 128:(tt + 1) * 128], c.PTB[0:1, 0:128],
                               rd=["PTB"], wr=["GRmu"])
                    P.mm(c.PS[4][:, 0:TS], c.s_ones_row_f[:], c.GR[0:1, 0:TS],
                         start=True, stop=True, rd=["GRmu", "c3"], wr=["PS4"])
                    P.copy(c.ET[1][:], c.PS[4][:, 0:TS], rd=["PS4"], wr=["ET1"])
                for f in range(KT):
                    for k in range(KT):
                        P.mm(c.PS[0][:], c.W1[:, k * 1024 + f * 128: k * 1024 + f * 128 + 128],
                             yb[:, k * 512:(k + 1) * 512], start=(k == 0), stop=(k == KT - 1),
                             rd=["W1g0", "W1g1", "A2"], wr=["PS0"])
                    P.act(c.ET[0][:], c.PS[0][:], AF.Silu, rd=["PS0"], wr=["ET0"])
                    for k in range(KT):
                        P.mm(c.PS[1][:], wu_t(k)[:, f * 128: f * 128 + 128],
                             yb[:, k * 512:(k + 1) * 512], start=(k == 0), stop=(k == KT - 1),
                             rd=WUK + ["A2"], wr=["PS1"])
                    P.tt(hb[:, f * 512:(f + 1) * 512], c.PS[1][:], c.ET[0][:], ALU.mult,
                         rd=["PS1", "ET0"], wr=["A3"])
                for dk in range(KT):
                    pb = c.PS[2 + dk % 2]
                    pk = f"PS{2 + dk % 2}"
                    for f in range(KT):
                        P.mm(pb[:], c.W2[:, f * 1024 + dk * 128: f * 1024 + dk * 128 + 128],
                             hb[:, f * 512:(f + 1) * 512], start=(f == 0), stop=(f == KT - 1),
                             rd=["W2g0", "W2g1", "A3"], wr=[pk])
                    accf = c.B[:].bitcast(F32)[:, dk * 2048 + s * 512: dk * 2048 + (s + 1) * 512]
                    if shared:
                        P.tt(accf, accf, pb[:], ALU.add, rd=[pk, f"B{dk}"], wr=[f"B{dk}"])
                    else:
                        eb = c.EB[dk % 2][:]
                        P.tt(eb, pb[:], c.ET[1][:], ALU.mult, rd=[pk, "ET1"], wr=[f"EB{dk%2}"])
                        P.tt(accf, accf, eb, ALU.add, rd=[f"EB{dk%2}", f"B{dk}"],
                             wr=[f"B{dk}"], q="pool")
        for s in range(4):
            t0 = th * TH + s * TS
            for dk in range(KT):
                accf = c.B[:].bitcast(F32)[:, dk * 2048 + s * 512: dk * 2048 + (s + 1) * 512]
                et1 = c.ET[1][:]
                P.dma("sp", et1, xin[dk * 128:(dk + 1) * 128, t0:t0 + TS], rd=["xres"], wr=["ET1"])
                P.stt(et1, accf, 0.5, et1, ALU.mult, ALU.add, rd=[f"B{dk}"], wr=["ET1"], q="dve")
                P.dma("sp", xout[dk * 128:(dk + 1) * 128, t0:t0 + TS], et1,
                      rd=["ET1"], wr=["xres"])
    P.fence("sp", ["xres"])


def emit_l3attn(P, c, W, xin, xout, uid):
    tag = f"l3a{uid}_"
    QKV = P.input("L3_attn_qkv_w", W["L3_attn_qkv_w"])
    bqc = c.sbuf_param(P, "L3a_bq", P.input("L3a_bq", W["L3_attn_bq"]), [128, 8])
    bkc = c.sbuf_param(P, "L3a_bk", P.input("L3a_bk", W["L3_attn_bk"]), [128, 8])
    bvc = c.sbuf_param(P, "L3a_bv", P.input("L3a_bv", W["L3_attn_bv"]), [128, 8])
    wpts_d = P.input("L3a_wpts", W["L3_attn_wpts"])
    wpts = c.W1[:, 0:2560]
    WPK = ["W1g0"]
    MV = P.input("L3_attn_mem_value_w", W["L3_attn_mem_value_w"])
    mvb = c.sbuf_param(P, "L3a_mvb", P.input("L3a_mvb", W["L3_attn_mvb"]), [128, 8])
    mgw = c.sbuf_param(P, "L3a_mgw", P.input("L3a_mgw", W["L3_attn_mgw"]), [128, 128], dtype=BF16)
    OW = P.input("L3_attn_out_w", W["L3_attn_out_w"])
    owb = c.sbuf_param(P, "L3a_owb", P.input("L3a_owb", W["L3_attn_owb"]), [128, 8])
    rows = c.sbuf_param(P, "L3a_rows", P.input("L3a_rows", W["L3_attn_rows"]), [1, 128])
    RK = ["pL3a_rows"]
    iot = c.sbuf_param(P, "c_iota1", P.input("c_iota1", W["c_iota1"]), [128, 2])
    IK = ["pc_iota1"]
    dif = c.sbuf_param(P, "c_dif", P.input("c_dif", W["c_dif"]), [128, 128])
    DK = ["pc_dif"]
    sutri = c.sbuf_param(P, "c_sutri", P.input("c_sutri", W["c_sutri"]), [128, 128])
    SK = ["pc_sutri"]
    ybuf = P.nc.dram_tensor(f"l3a_y{uid}", [D, T], BF16, kind="Internal").ap()
    qbuf = P.nc.dram_tensor(f"l3a_q{uid}", [D, T], BF16, kind="Internal").ap()
    kbuf = P.nc.dram_tensor(f"l3a_k{uid}", [D, T], BF16, kind="Internal").ap()
    vbuf = P.nc.dram_tensor(f"l3a_v{uid}", [D, T], BF16, kind="Internal").ap()
    sobuf = P.nc.dram_tensor(f"l3a_so{uid}", [D, T], BF16, kind="Internal").ap()
    xsf = b_f32(c, 0, KT * TS)
    P.dma("pool", c.W1[:, 0:12288].rearrange("p (k t) -> p k t", k=KT),
          QKV[:, 0:1536].rearrange("(k p) t -> p k t", p=128), wr=["W1g0", "W1g1", "W1g2"])
    P.dma("pool", c.W2[:, 0:12288].rearrange("p (k t) -> p k t", k=KT),
          QKV[:, 1536:3072].rearrange("(k p) t -> p k t", p=128), wr=["W2g0", "W2g1", "W2g2"])
    W1K = ["W1g0", "W1g1", "W1g2"]
    W2K = ["W2g0", "W2g1", "W2g2"]
    for s in range(NSLC):
        t0 = s * TS
        ytiles = [c.B[:, 2 * T + k * 512: 2 * T + k * 512 + 512] for k in range(KT)]
        ykeys = ["B2"] * KT
        emit_ln_slice(P, c, xin, t0, TS, xsf, ["B0", "B1"], ytiles, ykeys, tag + f"ln{s}_",
                      y_hbm=ybuf)
        for part, buf, bias, bk_ in ((0, qbuf, bqc, "pL3a_bq"), (1, kbuf, bkc, "pL3a_bk"),
                                     (2, vbuf, bvc, "pL3a_bv")):
            for f in range(KT):
                cidx = part * 1024 + f * 128
                for k in range(KT):
                    if cidx < 1536:
                        wt, wkk = c.W1[:, k * 1536 + cidx: k * 1536 + cidx + 128], W1K
                    else:
                        wt, wkk = c.W2[:, k * 1536 + cidx - 1536: k * 1536 + cidx - 1536 + 128], W2K
                    P.mm(c.PS[0][:], wt, ytiles[k], start=(k == 0), stop=(k == KT - 1),
                         rd=wkk + [ykeys[k]], wr=["PS0"])
                eb = c.EB[f % 2][:]
                if part == 0:
                    P.ts(eb, c.PS[0][:], bias[:, f:f + 1], 0.125, ALU.add, ALU.mult,
                         rd=["PS0", bk_], wr=[f"EB{f%2}"])
                else:
                    P.ts(eb, c.PS[0][:], bias[:, f:f + 1], None, ALU.add,
                         rd=["PS0", bk_], wr=[f"EB{f%2}"])
                P.dma("sp", buf[f * 128:(f + 1) * 128, t0:t0 + TS], eb,
                      rd=[f"EB{f%2}"], wr=[f"{tag}qkv{part}"])
    P.barrier("pool", ["A0", "A1", "A2", "A3", "A4", "A5", "A6", "A7",
                       "B0", "B1", "B2", "Bexp", "A3o", "A4k",
                       f"{tag}qkv0", f"{tag}qkv1", f"{tag}qkv2"])
    af = c.A[:].bitcast(F32)
    q_h = c.A[0:64, 0:4096]
    k_h = c.A[0:64, T:T + 4096]
    v_h = c.A[0:64, 2 * T:2 * T + 4096]
    v_tm = c.A[:, 3 * T:3 * T + 2048]
    o_sb = af[0:64, 7168:7680]
    kn = af[0:1, 8192:12288]
    for h in range(16):
        P.dma("sp", q_h, qbuf[h * 64:(h + 1) * 64, :], rd=[f"{tag}qkv0"], wr=["A0"])
        P.dma("sp", k_h, kbuf[h * 64:(h + 1) * 64, :], rd=[f"{tag}qkv1"], wr=["A1"])
        P.dma("sp", v_h, vbuf[h * 64:(h + 1) * 64, :], rd=[f"{tag}qkv2"], wr=["A2"])
        for i in range(32):
            P.trans(c.PTB[:, 0:64], v_h[:, i * 128:(i + 1) * 128], c.s_idb[0:64, 0:64],
                    rd=["A2", "c4"], wr=["PTB"])
            P.copy(v_tm[:, i * 64:(i + 1) * 64], c.PTB[:, 0:64], rd=["PTB"], wr=["A3"])
        for i in range(KT):
            sq = c.EB[i % 2][0:64, :]
            P.act(sq, k_h[:, i * 512:(i + 1) * 512], AF.Square, rd=["A1"], wr=[f"EB{i%2}"])
            P.mm(c.PS[4][0:1, 0:512], c.s_ones_col_b[0:64, :], sq, start=True, stop=True,
                 rd=[f"EB{i%2}", "c1"], wr=["PS4"])
            P.act(kn[:, i * 512:(i + 1) * 512], c.PS[4][0:1, 0:512], AF.Sqrt,
                  rd=["PS4"], wr=["A4k"])
        kmax = c.GR[0:1, TS:TS + 8]
        for s in range(NSLC):
            P.red(kmax[:, s:s + 1], kn[:, s * 512:(s + 1) * 512], ALU.max, rd=["A4k"], wr=["GRrstd"])
            if s > 0:
                P.tt(kmax[:, s:s + 1], kmax[:, s:s + 1], kmax[:, s - 1:s], ALU.max,
                     rd=["GRrstd"], wr=["GRrstd"])
        for s in range(NSLC):
            t0 = s * TS
            nsc = (s + 1) * 4
            sq = c.EB[0][0:64, :]
            P.act(sq, q_h[:, t0:t0 + TS], AF.Square, rd=["A0"], wr=["EB0"])
            P.mm(c.PS[4][0:1, 0:512], c.s_ones_col_b[0:64, :], sq, start=True, stop=True,
                 rd=["EB0", "c1"], wr=["PS4"])
            mrow = c.GR[0:1, 0:TS]
            P.act(mrow, c.PS[4][0:1, 0:512], AF.Sqrt, rd=["PS4"], wr=["GRmu"])
            P.ts(mrow, mrow, kmax[0:1, s:s + 1], None, ALU.mult, rd=["GRrstd", "GRmu"], wr=["GRmu"])
            nmrow = c.ET[0][0:1, 0:TS]
            P.ts(nmrow, mrow, -1.0, None, ALU.mult, rd=["GRmu"], wr=["ET0"])
            l_sb = c.CMB[0:1, 0:TS]
            P.memset(l_sb, 0.0, wr=["CMB"], q="dve")
            P.memset(o_sb, 0.0, wr=["A3o"], q="dve")
            for g in range((nsc + 7) // 8):
                ng = min(8, nsc - g * 8)
                for j in range(ng):
                    sc = g * 8 + j
                    P.mm(c.PS[0][:], k_h[:, sc * 128:(sc + 1) * 128], q_h[:, t0:t0 + TS],
                         start=True, stop=False, rd=["A1", "A0"], wr=["PS0"])
                    P.mm(c.PS[0][:], c.s_ones_row_f[:], nmrow, start=False, stop=True,
                         rd=["ET0", "c3"], wr=["PS0"])
                    if sc * 128 >= t0:
                        P.act(c.ET[1][:], c.PS[0][:], AF.Exp, rd=["PS0"], wr=["ET1"])
                        for sub in range(4):
                            tcol = t0 + sub * 128
                            if sc * 128 > tcol:
                                P.memset(c.ET[1][:, sub * 128:(sub + 1) * 128], 0.0,
                                         rd=["ET1"], wr=["ET1"], q="dve")
                            elif sc * 128 == tcol:
                                P.tt(c.ET[1][:, sub * 128:(sub + 1) * 128],
                                     c.ET[1][:, sub * 128:(sub + 1) * 128], sutri[:],
                                     ALU.mult, rd=["ET1"] + SK, wr=["ET1"])
                        P.copy(c.B[:, j * 512: j * 512 + 512], c.ET[1][:],
                               rd=["ET1"], wr=["Bexp"])
                    else:
                        P.act(c.B[:, j * 512: j * 512 + 512], c.PS[0][:], AF.Exp,
                              rd=["PS0"], wr=["Bexp"])
                for j in range(ng):
                    P.mm(c.PS[4][0:1, 0:512], c.s_ones_col_b[:], c.B[:, j * 512: j * 512 + 512],
                         start=(j == 0), stop=(j == ng - 1), rd=["Bexp", "c1"], wr=["PS4"])
                P.tt(l_sb, l_sb, c.PS[4][0:1, 0:512], ALU.add, rd=["PS4", "CMB"], wr=["CMB"])
                for j in range(ng):
                    sc = g * 8 + j
                    P.mm(c.PS[1][0:64, :], v_tm[:, sc * 64:(sc + 1) * 64],
                         c.B[:, j * 512: j * 512 + 512],
                         start=(j == 0), stop=(j == ng - 1), rd=["A3", "Bexp"], wr=["PS1"])
                P.tt(o_sb, o_sb, c.PS[1][0:64, :], ALU.add, rd=["PS1", "A3o"], wr=["A3o"])
            linv = c.CMB[0:1, 0:TS]
            P.recip(linv, l_sb, rd=["CMB"], wr=["CMB"])
            P.mm(c.PS[4][0:64, 0:512], c.s_ones_row_f[:, 0:64], linv, start=True, stop=True,
                 rd=["CMB", "c3"], wr=["PS4"])
            eb = c.EB[0][0:64, :]
            P.tt(eb, o_sb, c.PS[4][0:64, 0:512], ALU.mult, rd=["A3o", "PS4"], wr=["EB0"])
            P.dma("sp", sobuf[h * 64:(h + 1) * 64, t0:t0 + TS], eb, rd=["EB0"], wr=[f"{tag}so"])
    P.barrier("pool", ["A0", "A1", "A2", "A3", "A4", "A5", "A6", "A7", "A3o", "A4k",
                       "B0", "B1", "B2", "Bexp", f"{tag}so", "Bw", "Bg"])
    P.dma("pool", wpts, wpts_d, wr=["W1g0"])
    bf = c.B[:].bitcast(F32)
    lnd = c.GR[0:1, 0:16]
    P.act(lnd, rows[:, 32:48], AF.Sigmoid, rd=RK, wr=["GRmu"])
    P.act(lnd, lnd, AF.Ln, rd=["GRmu"], wr=["GRmu"])
    for h in range(16):
        P.mm(c.PTF[:, 0:1], c.s_ones_row_f[:], lnd[:, h:h + 1], start=True, stop=True,
             rd=["GRmu", "c3"], wr=["PTF"])
        lcol = c.ET[0][:, 0:1]
        P.copy(lcol, c.PTF[:, 0:1], rd=["PTF"], wr=["ET0"])
        wd = bf[:, h * 128:(h + 1) * 128]
        P.ts(wd, dif[:], lcol, None, ALU.mult, rd=DK + ["ET0"], wr=["Bw"])
        P.act(wd, wd, AF.Exp, rd=["Bw"], wr=["Bw"])
        msk = c.ET[0][:, 128:256]
        P.ts(msk, dif[:], 0.0, None, ALU.is_gt, rd=DK, wr=["ET0"])
        P.tt(wd, wd, msk, ALU.mult, rd=["Bw", "ET0"], wr=["Bw"])
        dup = c.B[:, 4640 + 2 * h: 4640 + 2 * h + 1]
        drev = c.B[:, 4704 + 2 * h: 4704 + 2 * h + 1]
        tmpc = c.ET[0][:, 64:65]
        P.tt(tmpc, iot[:, 0:1], lcol, ALU.mult, rd=IK + ["ET0"], wr=["ET0"])
        P.act(dup, tmpc, AF.Exp, rd=["ET0"], wr=["Bg"])
        P.tt(tmpc, iot[:, 1:2], lcol, ALU.mult, rd=IK + ["ET0"], wr=["ET0"])
        P.act(drev, tmpc, AF.Exp, rd=["ET0"], wr=["Bg"])
    d128 = c.GR[0:1, TS:TS + 16]
    P.ts(d128, lnd, 128.0, None, ALU.mult, rd=["GRmu"], wr=["GRrstd"])
    P.act(d128, d128, AF.Exp, rd=["GRrstd"], wr=["GRrstd"])
    d128c = bf[0:64, 2064:2080]
    P.mm(c.PTF[0:64, 0:16], c.s_ones_row_f[:, 0:64], d128, start=True, stop=True,
         rd=["GRrstd", "c3"], wr=["PTF"])
    P.copy(d128c, c.PTF[0:64, 0:16], rd=["PTF"], wr=["Bd128"])
    jb = bf[:, 2128:2164]
    P.mm(c.PTF[:, 0:36], c.s_ones_row_f[:], rows[:, 48:84], start=True, stop=True,
         rd=RK + ["c3"], wr=["PTF"])
    P.copy(jb, c.PTF[:, 0:36], rd=["PTF"], wr=["Bjb"])
    SC = bf[0:64, 2112:2128]
    P.memset(SC, 0.0, wr=["BSC"], q="dve")
    G = bf[:, 2176:2208]
    yw = c.A[:, 2 * T: 2 * T + 8 * 144]
    for i in range(32):
        ti = i * 128
        if i == 0:
            P.memset(yw[:, 0:8 * 144], 0.0, wr=["A2w"], q="dve")
            P.dma("sp", yw.rearrange("p (k t) -> p k t", k=KT)[:, :, 2:130],
                  ybuf[:, 0:128].rearrange("(k p) t -> p k t", p=128), rd=["A2w", "yhbm"], wr=["A2w"])
        else:
            P.dma("sp", yw.rearrange("p (k t) -> p k t", k=KT)[:, :, 0:130],
                  ybuf[:, ti - 2:ti + 128].rearrange("(k p) t -> p k t", p=128),
                  rd=["yhbm"], wr=["A2w"])
        ytl = lambda k, off: yw[:, k * 144 + 2 - off: k * 144 + 2 - off + 128]
        pts = c.A[:, 3 * T: 3 * T + 256]
        for pi, (proj, off) in enumerate((("w1", None), ("w2", 2), ("r1", 3), ("r2", 4))):
            ps = c.PS[0]
            if proj == "w1":
                for k in range(KT):
                    P.mm(ps[:, 0:64], ytl(k, 2), wpts[:, (0 * 8 + k) * 64:(0 * 8 + k) * 64 + 64],
                         start=(k == 0), stop=False, rd=["A2w"] + WPK, wr=["PS0"])
                for k in range(KT):
                    P.mm(ps[:, 0:64], ytl(k, 1), wpts[:, (1 * 8 + k) * 64:(1 * 8 + k) * 64 + 64],
                         start=False, stop=(k == KT - 1), rd=["A2w"] + WPK, wr=["PS0"])
            else:
                for k in range(KT):
                    P.mm(ps[:, 0:64], ytl(k, 0), wpts[:, (off * 8 + k) * 64:(off * 8 + k) * 64 + 64],
                         start=(k == 0), stop=(k == KT - 1), rd=["A2w"] + WPK, wr=["PS0"])
            P.copy(pts[:, pi * 64:(pi + 1) * 64], ps[:, 0:64], rd=["PS0"], wr=["A3p"])
        lines = c.A[:, 3 * T + 512: 3 * T + 512 + 2 * 192]
        for li, (pa, pb_) in enumerate(((0, 1), (2, 3))):
            ln = lines[:, li * 192: li * 192 + 96]
            p1 = lambda d: pts[:, pa * 64 + d:pa * 64 + 64:4]
            p2 = lambda d: pts[:, pb_ * 64 + d:pb_ * 64 + 64:4]
            PAIRS = [(0, 1), (0, 2), (0, 3), (1, 2), (1, 3), (2, 3)]
            nrm = c.ET[1][:, 0:16]
            for ci, (a, b) in enumerate(PAIRS):
                lc = ln[:, ci * 16:(ci + 1) * 16]
                t1 = c.ET[0][:, 300:316]
                P.tt(t1, p1(a), p2(b), ALU.mult, rd=["A3p"], wr=["ET0x"])
                t2 = c.ET[0][:, 316:332]
                P.tt(t2, p1(b), p2(a), ALU.mult, rd=["A3p"], wr=["ET0x"])
                P.tt(lc, t1, t2, ALU.subtract, rd=["ET0x"], wr=["A3l"])
                sqv = c.ET[0][:, 332:348]
                P.tt(sqv, lc, lc, ALU.mult, rd=["A3l"], wr=["ET0x"])
                if ci == 0:
                    P.copy(nrm, sqv, rd=["ET0x"], wr=["ET1"])
                else:
                    P.tt(nrm, nrm, sqv, ALU.add, rd=["ET0x", "ET1"], wr=["ET1"])
            P.act(nrm, nrm, AF.Sqrt, rd=["ET1"], wr=["ET1"])
            P.ts(nrm, nrm, 1e-12, None, ALU.max, rd=["ET1"], wr=["ET1"])
            P.recip(nrm, nrm, rd=["ET1"], wr=["ET1"])
            for ci in range(6):
                lc = ln[:, ci * 16:(ci + 1) * 16]
                P.tt(lc, lc, nrm, ALU.mult, rd=["A3l", "ET1"], wr=["A3l"])
        wl = lambda ci: lines[:, ci * 16:ci * 16 + 16]
        rl = lambda ci: lines[:, 192 + ci * 16:192 + ci * 16 + 16]
        for j in range(6):
            jw = lines[:, 96 + j * 16: 96 + (j + 1) * 16]
            for i6 in range(6):
                t1 = c.ET[0][:, 300:316]
                P.ts(t1, wl(i6), jb[:, i6 * 6 + j:i6 * 6 + j + 1], None, ALU.mult,
                     rd=["A3l", "Bjb"], wr=["ET0x"])
                if i6 == 0:
                    P.copy(jw, t1, rd=["ET0x"], wr=["A3l"])
                else:
                    P.tt(jw, jw, t1, ALU.add, rd=["ET0x"], wr=["A3l"])
        jwl = lambda ci: lines[:, 96 + ci * 16:96 + ci * 16 + 16]
        rlT = c.A[:, 3 * T + 1024: 3 * T + 1024 + 4 * 128]
        jwT = c.A[:, 3 * T + 1536: 3 * T + 1536 + 4 * 128]
        pad = c.EB[0][:]
        for grp in range(4):
            for which, dstT in ((0, rlT), (1, jwT)):
                P.memset(pad[:, 0:128], 0.0, wr=["EB0"], q="dve")
                for ci in range(6):
                    src = rl(ci) if which == 0 else jwl(ci)
                    for hh in range(4):
                        P.copy(pad[:, hh * 32 + ci: hh * 32 + ci + 1],
                               src[:, grp * 4 + hh: grp * 4 + hh + 1],
                               rd=["A3l"], wr=["EB0"])
                P.trans(c.PTB[:, 0:128], pad[:, 0:128], c.s_idb[:], rd=["EB0", "c4"], wr=["PTB"])
                P.copy(dstT[:, grp * 128:(grp + 1) * 128], c.PTB[:, 0:128], rd=["PTB"], wr=["A3T"])
        ms = c.ET[1][:, 0:16]
        for h in range(16):
            grp, hh = h // 4, h % 4
            lt = rlT[:, grp * 128:(grp + 1) * 128][hh * 32:hh * 32 + 32, :]
            jt = jwT[:, grp * 128:(grp + 1) * 128][hh * 32:hh * 32 + 32, :]
            P.mm(c.PS[1][:, 0:128], lt, jt, start=True, stop=True, rd=["A3T"], wr=["PS1"],
                 tile_position=(hh * 32, 0))
            sqv2 = c.ET[0][:, 0:128]
            P.act(sqv2, c.PS[1][:, 0:128], AF.Square, rd=["PS1"], wr=["ET0x"])
            P.tt(sqv2, sqv2, bf[:, h * 128:(h + 1) * 128], ALU.mult, rd=["ET0x", "Bw"], wr=["ET0x"])
            P.red(ms[:, h:h + 1], sqv2, ALU.add, rd=["ET0x"], wr=["ET1"])
        RF = c.A[:, 3 * T + 2048 + 512: 3 * T + 2048 + 512 + 1024]
        WF = c.A[:, 3 * T + 2048 + 1536: 3 * T + 2048 + 1536 + 1024]
        P.memset(RF, 0.0, wr=["A3R"], q="pool")
        P.memset(WF, 0.0, wr=["A3W"], q="pool")
        pi = 0
        for i6 in range(6):
            for j6 in range(6):
                P.tt(RF[:, pi::64], rl(i6), rl(j6), ALU.mult, rd=["A3l"], wr=["A3R"])
                P.tt(WF[:, pi::64], jwl(i6), jwl(j6), ALU.mult, rd=["A3l"], wr=["A3W"])
                pi += 1
        for h in range(16):
            P.trans(c.PTB[0:64, 0:128], RF[:, h * 64:(h + 1) * 64], c.s_idb[:],
                    rd=["A3R", "c4"], wr=["PTB"])
            rT = c.EB[1][:]
            P.copy(rT[0:64, 0:128], c.PTB[0:64, 0:128], rd=["PTB"], wr=["EB1"])
            scb = c.B[0:64, 4608 + 2 * h: 4608 + 2 * h + 1]
            P.copy(scb, SC[:, h:h + 1], rd=["BSC"], wr=["BSCb"])
            P.mm(c.PS[0][:, 0:1], rT[0:64, 0:128], scb, start=True, stop=True,
                 rd=["EB1", "BSCb"], wr=["PS0"])
            mc = c.ET[0][:, 0:1]
            P.tt(mc, c.PS[0][:, 0:1], c.B[:, 4640 + 2 * h: 4640 + 2 * h + 1], ALU.mult,
                 rd=["PS0", "Bg"], wr=["ET0x"])
            P.tt(ms[:, h:h + 1], ms[:, h:h + 1], mc, ALU.add, rd=["ET0x", "ET1"], wr=["ET1"])
            P.mm(c.PS[0][0:64, 0:1], WF[:, h * 64:(h + 1) * 64],
                 c.B[:, 4704 + 2 * h: 4704 + 2 * h + 1], start=True, stop=True,
                 rd=["A3W", "Bg"], wr=["PS0"])
            P.ts(SC[:, h:h + 1], SC[:, h:h + 1], d128c[:, h:h + 1], None, ALU.mult,
                 rd=["BSC", "Bd128"], wr=["BSC"])
            P.tt(SC[:, h:h + 1], SC[:, h:h + 1], c.PS[0][0:64, 0:1], ALU.add,
                 rd=["PS0", "BSC"], wr=["BSC"])
        for k in range(KT):
            P.mm(c.PS[1][:, 0:16], ytl(k, 0), mgw[:, k * 16:(k + 1) * 16],
                 start=(k == 0), stop=False, rd=["A2w", "pL3a_mgw"], wr=["PS1"])
        P.mm(c.PS[1][:, 0:16], c.s_ones_row_f[:], rows[:, 0:16], start=False, stop=True,
             rd=RK + ["c3"], wr=["PS1"])
        gt = c.ET[0][:, 0:16]
        P.act(gt, c.PS[1][:, 0:16], AF.Sigmoid, rd=["PS1"], wr=["ET0x"])
        if i == 0:
            P.mm(c.PTF[:, 0:16], c.s_ones_row_f[:], rows[:, 16:32], start=True, stop=True,
                 rd=RK + ["c3"], wr=["PTF"])
            P.copy(c.EBI[:], c.PTF[:, 0:16], rd=["PTF"], wr=["EBI"])
        msg = c.ET[1][:, 32:48]
        P.tt(msg, ms, c.EBI[:], ALU.mult, rd=["ET1", "EBI"], wr=["ET1"])
        P.act(msg, msg, AF.Sigmoid, rd=["ET1"], wr=["ET1"])
        P.tt(msg, msg, gt, ALU.mult, rd=["ET1", "ET0x"], wr=["ET1"])
        P.red(G[:, i:i + 1], msg, ALU.add, rd=["ET1"], wr=["BG"])
        P.ts(G[:, i:i + 1], G[:, i:i + 1], 1.0 / 16.0, None, ALU.mult, rd=["BG"], wr=["BG"])
    P.barrier("pool", ["A2w", "A3p", "A3l", "A3T", "A3R", "A3W", "A3c", "ET0x", "ET1",
                       "EB0", "EB1"])
    P.dma("pool", c.W1[:, 0:8192].rearrange("p (k t) -> p k t", k=KT),
          MV.rearrange("(k p) t -> p k t", p=128), wr=["W1g0", "W1g1"])
    P.dma("pool", c.W2[:, 0:8192].rearrange("p (k t) -> p k t", k=KT),
          OW.rearrange("(k p) t -> p k t", p=128), wr=["W2g0", "W2g1"])
    for s in range(NSLC):
        t0 = s * TS
        yb = c.A[:, 2 * T: 2 * T + 4096]
        P.dma("sp", yb.rearrange("p (k t) -> p k t", k=KT),
              ybuf[:, t0:t0 + TS].rearrange("(k p) t -> p k t", p=128), rd=["yhbm"], wr=["A2w"])
        for tt in range(4):
            i = s * 4 + tt
            P.copy(c.EB[0][:, 0:1], G[:, i:i + 1], rd=["BG"], wr=["EB0"])
            P.trans(c.PTB[0:1, 0:128], c.EB[0][:, 0:1], c.s_idb[:], rd=["EB0", "c4"], wr=["PTB"])
            P.copy(c.GR[0:1, tt * 128:(tt + 1) * 128], c.PTB[0:1, 0:128], rd=["PTB"], wr=["GRmu"])
        P.mm(c.PS[4][:, 0:TS], c.s_ones_row_f[:], c.GR[0:1, 0:TS], start=True, stop=True,
             rd=["GRmu", "c3"], wr=["PS4"])
        gsb = c.ET[1][:]
        P.copy(gsb, c.PS[4][:, 0:TS], rd=["PS4"], wr=["ET1"])
        cmb = c.A[:, 3 * T: 3 * T + 4096]
        for f in range(KT):
            for k in range(KT):
                P.mm(c.PS[0][:], c.W1[:, k * 1024 + f * 128: k * 1024 + f * 128 + 128],
                     yb[:, k * 512:(k + 1) * 512], start=(k == 0), stop=(k == KT - 1),
                     rd=["W1g0", "W1g1", "A2w"], wr=["PS0"])
            mv_t = c.ET[0][:]
            P.ts(mv_t, c.PS[0][:], mvb[:, f:f + 1], None, ALU.add,
                 rd=["PS0", "pL3a_mvb"], wr=["ET0x"])
            P.tt(mv_t, mv_t, gsb, ALU.mult, rd=["ET0x", "ET1"], wr=["ET0x"])
            so_t = c.EB[0][:]
            P.dma("sp", so_t, sobuf[f * 128:(f + 1) * 128, t0:t0 + TS], rd=[f"{tag}so"], wr=["EB0"])
            P.tt(cmb[:, f * 512:(f + 1) * 512], mv_t, so_t, ALU.add,
                 rd=["ET0x", "EB0"], wr=["A3c"])
        for dk in range(KT):
            pb = c.PS[2 + dk % 2]
            pk = f"PS{2 + dk % 2}"
            for f in range(KT):
                P.mm(pb[:], c.W2[:, f * 1024 + dk * 128: f * 1024 + dk * 128 + 128],
                     cmb[:, f * 512:(f + 1) * 512], start=(f == 0), stop=(f == KT - 1),
                     rd=["W2g0", "W2g1", "A3c"], wr=[pk])
            e32 = c.ET[0][:]
            P.ts(e32, pb[:], owb[:, dk:dk + 1], None, ALU.add, rd=[pk, "pL3a_owb"], wr=["ET0x"])
            xet = c.ET[1][:]
            P.dma("sp", xet, xin[dk * 128:(dk + 1) * 128, t0:t0 + TS], rd=["xres"], wr=["ET1"])
            P.tt(xet, xet, e32, ALU.add, rd=["ET0x"], wr=["ET1"])
            P.dma("sp", xout[dk * 128:(dk + 1) * 128, t0:t0 + TS], xet, rd=["ET1"],
                  wr=["xres"])
    P.fence("sp", ["xres"])




def emit_l2attn(P, c, W, xin, xout, uid):
    tag = f"l2a{uid}_"
    IPW = P.input("L2_attn_ipWT", W["L2_attn_ipWT"])
    OPW = P.input("L2_attn_opWT", W["L2_attn_opWT"])
    lnp = c.sbuf_param(P, "L2a_lnp", P.input("L2a_lnp", W["L2_attn_lnp"]), [128, 16])
    cw = c.sbuf_param(P, "L2a_conv", P.input("L2a_conv", W["L2_attn_convc"]), [128, 85])
    dtb = c.sbuf_param(P, "L2a_dtb", P.input("L2a_dtb", W["L2_attn_dtb"]), [32, 1])
    alog = c.sbuf_param(P, "L2a_alog", P.input("L2a_alog", W["L2_attn_alog"]), [32, 1])
    drow = c.sbuf_param(P, "L2a_drow", P.input("L2a_drow", W["L2_attn_drow"]), [1, 32])
    nrow_d = P.input("L2a_nrow", W["L2_attn_nrow"])
    srep_d = P.input("c_srep32b", W["c_srep32"], dtype=BF16)
    srepf_d = P.input("c_srep32f", W["c_srep32"])
    srepi_d = P.input("c_srepi", W["c_srepi"])
    triuc_d = P.input("c_triuc", W["c_triuc"])
    idf32 = c.sbuf_param(P, "L2a_idf32", P.input("c_idf32", W["c_idf32"]), [32, 32])
    SRK = ["W1srep"]; SFK = ["W1g0", "W1g1"]; SIK = ["W2g0"]; TCK = ["W2g1"]
    IDK = ["pL2a_idf32"]
    knd = "ExternalOutput" if W.get("__l2adbg__") else "Internal"
    xnbuf = P.nc.dram_tensor(f"l2a_xn{uid}", [D, T], BF16, kind="Internal").ap()
    ypbuf = P.nc.dram_tensor(f"l2a_yp{uid}", [D, T], BF16, kind="Internal").ap()
    zbuf = P.nc.dram_tensor(f"l2a_z{uid}", [T, 2048], BF16, kind=knd).ap()
    xbcbuf = P.nc.dram_tensor(f"l2a_xbc{uid}", [2176, T], BF16, kind=knd).ap()
    xstbuf = P.nc.dram_tensor(f"l2a_xst{uid}", [T, 2048], BF16, kind=knd).ap()
    xdtbuf = P.nc.dram_tensor(f"l2a_xdt{uid}", [T, 2048], BF16, kind=knd).ap()
    ytmbuf = P.nc.dram_tensor(f"l2a_ytm{uid}", [T, 2048], BF16, kind=knd).ap()
    aflatbuf = P.nc.dram_tensor(f"l2a_afl{uid}", [32, T], F32, kind=knd).ap()
    af = c.A[:].bitcast(F32)
    acum = af[0:32, 0:4096]
    dtf = af[32:64, 0:4096]
    P.barrier("pool", ["A0", "A1", "A2", "A3", "A4", "A5", "A6", "A7", "A0c", "A1c", "Adt"])
    xsf = b_f32(c, 0, KT * TS)
    P.dma("pool", c.W1[:, 0:8192].rearrange("p (k t) -> p k t", k=KT),
          IPW[:, 0:1024].rearrange("(k p) t -> p k t", p=128), wr=["W1g0", "W1g1"])
    P.dma("pool", c.W2[:, 0:8192].rearrange("p (k t) -> p k t", k=KT),
          IPW[:, 1024:2048].rearrange("(k p) t -> p k t", p=128), wr=["W2g0", "W2g1"])
    for s in range(NSLC):
        t0 = s * TS
        ytiles = [c.B[:, 2 * T + k * 512: 2 * T + k * 512 + 512] for k in range(KT)]
        ykeys = ["B2"] * KT
        emit_ln_slice(P, c, xin, t0, TS, xsf, ["B0", "B1"], ytiles, ykeys, tag + f"ln{s}_",
                      y_hbm=xnbuf, gcol=lnp[:, 0:8], bcol_=lnp[:, 8:16], gb_keys=["pL2a_lnp"],
                      y_hbm_plain=ypbuf)
        for tt in range(4):
            for chk in range(4):
                warena = c.W1 if chk < 2 else c.W2
                wkk = ["W1g0", "W1g1"] if chk < 2 else ["W2g0", "W2g1"]
                cbase = (chk % 2) * 512
                for k in range(KT):
                    P.mm(c.PS[0][:], ytiles[k][:, tt * 128:(tt + 1) * 128],
                         warena[:, k * 1024 + cbase: k * 1024 + cbase + 512],
                         start=(k == 0), stop=(k == KT - 1), rd=wkk + [ykeys[k]], wr=["PS0"])
                eb = c.EB[chk % 2][:]
                P.act(eb, c.PS[0][:], AF.Silu, rd=["PS0"], wr=[f"EB{chk%2}"])
                P.dma("sp", zbuf[t0 + tt * 128: t0 + (tt + 1) * 128, chk * 512:(chk + 1) * 512],
                      eb[:, 0:512], rd=[f"EB{chk%2}"], wr=["zb"])
    P.dma("pool", c.W1[:, 0:9216].rearrange("p (k t) -> p k t", k=KT),
          IPW[:, 2048:3200].rearrange("(k p) t -> p k t", p=128), wr=["W1g0", "W1g1", "W1g2"])
    P.dma("pool", c.W2[:, 0:8448].rearrange("p (k t) -> p k t", k=KT),
          IPW[:, 3200:4256].rearrange("(k p) t -> p k t", p=128), wr=["W2g0", "W2g1", "W2g2"])
    W1K3 = ["W1g0", "W1g1", "W1g2"]; W2K3 = ["W2g0", "W2g1", "W2g2"]
    srep = c.W1[0:32, 9216:11264]
    P.dma("pool", srep, srep_d, wr=["W1srep"])

    def ipw2(k, cidx):
        if cidx < 1152:
            return c.W1[:, k * 1152 + cidx:], W1K3
        return c.W2[:, k * 1056 + cidx - 1152:], W2K3
    bcm = c.A[0:64, 8192 + 2048 + 128:0] if False else None
    for s in range(NSLC):
        t0 = s * TS
        ytiles = [c.B[:, 2 * T + k * 512: 2 * T + k * 512 + 512] for k in range(KT)]
        ykeys = ["B2"] * KT
        P.dma("sp", c.B[:, 2 * T:2 * T + 4096].rearrange("p (k t) -> p k t", k=KT),
              xnbuf[:, t0:t0 + TS].rearrange("(k p) t -> p k t", p=128),
              rd=["yhbm"], wr=["B2"])
        cwin = c.B[:, 3 * T:3 * T + 17 * 520]
        for f in range(17):
            for k in range(KT):
                wt, wkk = ipw2(k, f * 128)
                P.mm(c.PS[0][:], wt[:, 0:128], ytiles[k], start=(k == 0), stop=(k == KT - 1),
                     rd=wkk + [ykeys[k]], wr=["PS0"])
            eb = c.EB[f % 2][:]
            P.copy(eb, c.PS[0][:], rd=["PS0"], wr=[f"EB{f%2}"])
            if s == 0:
                P.memset(cwin[:, f * 520:f * 520 + 3], 0.0, wr=["B3w"], q="dve")
            else:
                P.dma("sp", cwin[:, f * 520:f * 520 + 3],
                      xbcbuf[f * 128:(f + 1) * 128, t0 - 3:t0], rd=["xbcb"], wr=["B3w"])
            P.copy(cwin[:, f * 520 + 3:f * 520 + 515], eb, rd=[f"EB{f%2}"], wr=["B3w"])
            P.dma("sp", xbcbuf[f * 128:(f + 1) * 128, t0:t0 + TS], eb, rd=[f"EB{f%2}"], wr=["xbcb"])
        for k in range(KT):
            wt, wkk = ipw2(k, 2176)
            P.mm(c.PS[1][0:32, :], wt[:, 0:32], ytiles[k], start=(k == 0), stop=(k == KT - 1),
                 rd=wkk + [ykeys[k]], wr=["PS1"])
        v0 = c.ET[0][0:32, :]
        P.ts(v0, c.PS[1][0:32, :], dtb[:], None, ALU.add, rd=["PS1", "pL2a_dtb"], wr=["ET0"])
        t1 = c.ET[1][0:32, :]
        P.act(t1, v0, AF.Abs, rd=["ET0"], wr=["ET1"])
        P.act(t1, t1, AF.Exp, scale=-1.0, rd=["ET1"], wr=["ET1"])
        eps1, epsk1 = c.ccol(1.0)
        P.act(t1, t1, AF.Ln, bias=eps1[0:32, :], rd=["ET1", epsk1], wr=["ET1"])
        P.ts(v0, v0, 0.0, None, ALU.max, rd=["ET0"], wr=["ET0"])
        dt_s = c.EB[0][0:32, :]
        P.tt(dt_s, v0, t1, ALU.add, rd=["ET0", "ET1"], wr=["EB0"])
        P.copy(dtf[:, t0:t0 + TS][0:32, :], dt_s, rd=["EB0"], wr=["Adt"])
        acol_ = c.ET[0][0:32, 0:1]
        P.act(acol_, alog[:], AF.Exp, rd=["pL2a_alog"], wr=["ET0"])
        adt = c.ET[1][0:32, :]
        P.ts(adt, dt_s, acol_, None, ALU.mult, rd=["EB0", "ET0"], wr=["ET1"])
        P.ts(adt, adt, -1.0, None, ALU.mult, rd=["ET1"], wr=["ET1"])
        ones32 = c.EB[1][0:32, 0:64]
        P.memset(ones32, 1.0, wr=["EB1"], q="dve")
        for ch8 in range(8):
            P.scan(acum[:, t0 + ch8 * 64: t0 + (ch8 + 1) * 64], ones32,
                   adt[:, ch8 * 64:(ch8 + 1) * 64], 0.0, rd=["ET1", "EB1"], wr=["A0c", "A1c"])
        cout = c.B[:, 3 * T + 8840 + 504:0] if False else c.B[:, 3 * T + 8840: 3 * T + 8840 + 8704]
        for ch in range(17):
            w0 = cwin[:, ch * 520:ch * 520 + 515]
            e32 = c.ET[0][:]
            P.ts(e32, w0[:, 0:512], cw[:, ch * 4:ch * 4 + 1], None, ALU.mult,
                 rd=["B3w", "pL2a_conv"], wr=["ET0"])
            for j in range(1, 4):
                eb2 = c.ET[1][:]
                P.ts(eb2, w0[:, j:j + 512], cw[:, ch * 4 + j:ch * 4 + j + 1], None, ALU.mult,
                     rd=["B3w", "pL2a_conv"], wr=["ET1"])
                P.tt(e32, e32, eb2, ALU.add, rd=["ET0", "ET1"], wr=["ET0"])
            P.ts(e32, e32, cw[:, 68 + ch:69 + ch], None, ALU.add, rd=["ET0", "pL2a_conv"], wr=["ET0"])
            P.act(cout[:, ch * 512:(ch + 1) * 512], e32, AF.Silu, rd=["ET0"], wr=["B4c"])
        bcm_t = c.A[0:64, 10240:10240 + 4096]
        ccm_t = c.A[0:64, 14336:14336 + 4096]
        P.copy(bcm_t[:, t0:t0 + TS], cout[0:64, 16 * 512:17 * 512], rd=["B4c"], wr=["Abc"])
        P.copy(ccm_t[:, t0:t0 + TS], cout[64:128, 16 * 512:17 * 512], rd=["B4c"], wr=["Acc"])
        for tt in range(4):
            xst = c.B[:, 0:2048]
            for ch in range(16):
                P.trans(c.PTB[:, 0:128], cout[:, ch * 512 + tt * 128: ch * 512 + (tt + 1) * 128],
                        c.s_idb[:], rd=["B4c", "c4"], wr=["PTB"])
                P.copy(xst[:, ch * 128:(ch + 1) * 128], c.PTB[:, 0:128], rd=["PTB"], wr=["B0"])
            P.dma("sp", xstbuf[t0 + tt * 128:t0 + (tt + 1) * 128, :], xst, rd=["B0"], wr=["xstb"])
            xdt_sb = c.B[:, 2048:4096]
            for piece in range(4):
                pb = c.PS[2 + piece % 2]
                pk = f"PS{2 + piece % 2}"
                P.mm(pb[:], c.EB[0][0:32, tt * 128:(tt + 1) * 128],
                     srep[:, piece * 512:(piece + 1) * 512],
                     start=True, stop=True, rd=["EB0"] + SRK, wr=[pk])
                P.tt(xdt_sb[:, piece * 512:(piece + 1) * 512],
                     xst[:, piece * 512:(piece + 1) * 512], pb[:], ALU.mult,
                     rd=["B0", pk], wr=["B1"])
            P.dma("sp", xdtbuf[t0 + tt * 128:t0 + (tt + 1) * 128, :], xdt_sb, rd=["B1"], wr=["xdtb"])
    P.dma("sp", aflatbuf[:, :], acum, rd=["A0c", "A1c"], wr=["aflat"])
    srepf = c.W1[:].bitcast(F32)[0:32, 0:2048]
    P.dma("sp", srepf, srepf_d, wr=["W1g0", "W1g1"])
    srepi = c.W2[0:64, 0:2048]
    P.dma("pool", srepi, srepi_d, wr=["W2g0"])
    triuc = c.W2[0:64, 4096:6144]
    P.dma("pool", triuc, triuc_d, wr=["W2g1"])
    Wm = c.A[0:64, 18432:18432 + 2048]
    CBs = c.A[0:64, 20480:20544]
    Btm = c.A[0:64, 20544:20608]
    acT = af[0:64, 10304:10336]
    dec = af[0:64, 10336:10368]
    cdc = af[0:64, 10368:10400]
    eai = af[0:64, 10400:10432]
    neg = af[64:96, 5120:5152]
    aflr = c.B[:].bitcast(F32)[0:1, 0:2048]
    state = af[0:64, 10432:10432 + 2048]
    stateb = c.A[0:64, 25088:25088 + 2048]
    Xc = c.A[0:64, 27136:27136 + 2048]
    xsc = c.A[0:64, 29184:29184 + 2048]
    ysb = c.A[0:64, 31232 - 512:0] if False else c.B[:, 4096:4096 + 2048][0:64, :]
    P.memset(state, 0.0, wr=["Ast"], q="dve")
    P.memset(stateb, 0.0, wr=["Astb"], q="pool")
    P.mm(c.PTF[0:64, 0:32], c.s_ones_row_f[:, 0:64], drow[:], start=True, stop=True,
         rd=["pL2a_drow", "c3"], wr=["PTF"])
    dsk = af[0:64, 10432 + 2048:10432 + 2048 + 32]
    P.copy(dsk, c.PTF[0:64, 0:32], rd=["PTF"], wr=["Adsk"])
    for ci in range(64):
        c0 = ci * 64
        P.dma("sp", Xc, xdtbuf[c0:c0 + 64, :], rd=["xdtb"], wr=["AX"])
        P.dma("sp", xsc, xstbuf[c0:c0 + 64, :], rd=["xstb"], wr=["Axs"])
        P.dma("sp", aflr.rearrange("o (h t) -> o h t", h=32),
              aflatbuf[:, c0:c0 + 64], rd=["aflat"], wr=["B0"])
        P.mm(c.PTF[0:64, 0:32], acum[:, c0:c0 + 64], idf32[:], start=True, stop=True,
             rd=["A0c", "A1c"] + IDK, wr=["PTF"])
        P.copy(acT, c.PTF[0:64, 0:32], rd=["PTF"], wr=["AacT"])
        P.mm(c.PTF[0:1, 0:32], acum[:, c0 + 63:c0 + 64], idf32[:], start=True, stop=True,
             rd=["A0c", "A1c"] + IDK, wr=["PTF"])
        alr = c.GR[0:1, 0:32]
        P.copy(alr, c.PTF[0:1, 0:32], rd=["PTF"], wr=["GRmu"])
        P.mm(c.PTF[0:64, 0:32], c.s_ones_row_f[:, 0:64], alr, start=True, stop=True,
             rd=["GRmu", "c3"], wr=["PTF"])
        e1 = c.ET[0][0:64, 0:32]
        P.tt(e1, c.PTF[0:64, 0:32], acT, ALU.subtract, rd=["PTF", "AacT"], wr=["ET0"])
        P.act(dec, e1, AF.Exp, rd=["ET0"], wr=["Adec"])
        P.act(cdc, c.PTF[0:64, 0:32], AF.Exp, rd=["PTF"], wr=["Acd"])
        P.act(eai, acT, AF.Exp, rd=["AacT"], wr=["Aexp"])
        P.ts(neg, acum[:, c0:c0 + 64][0:32, :], -1.0, None, ALU.mult, rd=["A0c", "A1c"], wr=["Aneg"]) if False else None
        negc = c.ET[1][0:32, 0:64]
        P.ts(negc, acum[:, c0:c0 + 64][0:32, :], -1.0, None, ALU.mult, rd=["A0c", "A1c"], wr=["ET1"])
        for piece in range(4):
            pb = c.PS[piece]
            pk = f"PS{piece}"
            P.mm(pb[0:64, :], negc, srepf[:, piece * 512:(piece + 1) * 512],
                 start=True, stop=False, rd=["ET1"] + SFK, wr=[pk])
            P.mm(pb[0:64, :], c.s_ones_row_f[:, 0:64], aflr[:, piece * 512:(piece + 1) * 512],
                 start=False, stop=True, rd=["B0", "c3"], wr=[pk])
            eclp = c.ET[0][0:64, 0:512]
            P.ts(eclp, pb[0:64, :], 0.0, None, ALU.min, rd=[pk], wr=["ET0"])
            P.act(Wm[:, piece * 512:(piece + 1) * 512], eclp, AF.Exp,
                  rd=["ET0"], wr=["AWm"])
            P.tt(Wm[:, piece * 512:(piece + 1) * 512], Wm[:, piece * 512:(piece + 1) * 512],
                 triuc[:, piece * 512:(piece + 1) * 512], ALU.mult,
                 rd=["AWm"] + TCK, wr=["AWm"])
        P.mm(c.PTF[0:64, 0:64], ccm_t[:, c0:c0 + 64], bcm_t[:, c0:c0 + 64],
             start=True, stop=True, rd=["Acc", "Abc"], wr=["PTF"])
        P.copy(CBs, c.PTF[0:64, 0:64], rd=["PTF"], wr=["ACB"])
        for piece in range(4):
            pb = c.PS[piece]
            pk = f"PS{piece}"
            P.mm(pb[0:64, :], CBs, srepi[:, piece * 512:(piece + 1) * 512],
                 start=True, stop=True, rd=["ACB"] + SIK, wr=[pk])
            P.tt(Wm[:, piece * 512:(piece + 1) * 512], Wm[:, piece * 512:(piece + 1) * 512],
                 pb[0:64, :], ALU.mult, rd=["AWm", pk], wr=["AWm"])
        P.trans(c.PTB[0:64, 0:64], bcm_t[:, c0:c0 + 64], c.s_idb[0:64, 0:64],
                rd=["Abc", "c4"], wr=["PTB"])
        P.copy(Btm, c.PTB[0:64, 0:64], rd=["PTB"], wr=["ABtm"])
        for hg in range(2):
            for hh in range(16):
                h = hg * 16 + hh
                hc = slice(h * 64, (h + 1) * 64)
                py = c.PS[0 + hh // 8]
                pyk = f"PS{0 + hh // 8}"
                po = c.PS[2 + hh // 8]
                pok = f"PS{2 + hh // 8}"
                pst = c.PS[4 + hh // 8]
                pstk = f"PS{4 + hh // 8}"
                ppos = (hh % 8) * 64
                P.mm(py[0:64, ppos:ppos + 64], Wm[:, h * 64:(h + 1) * 64], Xc[:, hc],
                     start=True, stop=True, rd=["AWm", "AX"], wr=[pyk])
                P.mm(po[0:64, ppos:ppos + 64], ccm_t[:, c0:c0 + 64], stateb[:, hc],
                     start=True, stop=True, rd=["Acc", "Astb"], wr=[pok])
                P.ts(Xc[:, hc], Xc[:, hc], dec[:, h:h + 1], None, ALU.mult,
                     rd=["AX", "Adec"], wr=["AX"])
                P.mm(pst[0:64, ppos:ppos + 64], Btm, Xc[:, hc],
                     start=True, stop=True, rd=["ABtm", "AX"], wr=[pstk])
                P.ts(state[:, hc], state[:, hc], cdc[:, h:h + 1], None, ALU.mult,
                     rd=["Ast", "Acd"], wr=["Ast"])
                P.tt(state[:, hc], state[:, hc], pst[0:64, ppos:ppos + 64], ALU.add,
                     rd=["Ast", pstk], wr=["Ast"])
                P.copy(stateb[:, hc], state[:, hc], rd=["Ast"], wr=["Astb"], q="pool")
            for hh in range(16):
                h = hg * 16 + hh
                hc = slice(h * 64, (h + 1) * 64)
                py = c.PS[0 + hh // 8]; pyk = f"PS{0 + hh // 8}"
                po = c.PS[2 + hh // 8]; pok = f"PS{2 + hh // 8}"
                ppos = (hh % 8) * 64
                e1 = c.ET[0][0:64, 0:64]
                P.ts(e1, po[0:64, ppos:ppos + 64], eai[:, h:h + 1], None, ALU.mult,
                     rd=[pok, "Aexp"], wr=["ET0"])
                P.tt(e1, e1, py[0:64, ppos:ppos + 64], ALU.add, rd=["ET0", pyk], wr=["ET0"])
                e2 = c.ET[1][0:64, 0:64]
                P.ts(e2, xsc[:, hc], dsk[:, h:h + 1], None, ALU.mult,
                     rd=["Axs", "Adsk"], wr=["ET1"])
                P.tt(ysb[:, hc], e1, e2, ALU.add, rd=["ET0", "ET1"], wr=["By"])
        P.dma("sp", ytmbuf[c0:c0 + 64, :], ysb, rd=["By"], wr=["ytmb"])
    P.barrier("pool", ["B0", "B1", "B2", "B3w", "B4c", "B0", "B1", "By",
                       "Byl", "Bz", "Byg", "Bygc"])
    P.dma("pool", c.W1[:, 0:8192].rearrange("p (k t) -> p k t", k=KT),
          OPW[0:1024, :].rearrange("(k p) t -> p k t", p=128), wr=["W1g0", "W1g1"])
    P.dma("pool", c.W2[:, 0:8192].rearrange("p (k t) -> p k t", k=KT),
          OPW[1024:2048, :].rearrange("(k p) t -> p k t", p=128), wr=["W2g0", "W2g1"])
    nrow = c.B[:].bitcast(F32)[0:1, 9216:11264]
    P.dma("sp", nrow, nrow_d, wr=["Bnr"])
    nwb = c.B[:, 6144:6144 + 4096]
    for piece in range(4):
        P.mm(c.PS[4][:, 0:512], c.s_ones_row_f[:], nrow[:, piece * 512:(piece + 1) * 512],
             start=True, stop=True, rd=["Bnr", "c3"], wr=["PS4"])
        P.copy(nwb[:, piece * 1024:piece * 1024 + 1024][:, 0:1024].bitcast(F32)[:, 0:512] if False
               else nwb[:, piece * 1024:piece * 1024 + 1024].bitcast(F32)[:, 0:512],
               c.PS[4][:, 0:512], rd=["PS4"], wr=["Bnw"])
    nwbf = nwb.bitcast(F32)
    for s in range(NSLC):
        t0 = s * TS
        ygc = c.B[:, 10240:10240 + 8192]
        for tt in range(4):
            ti = t0 + tt * 128
            yl = c.B[:, 0:2048]
            P.dma("sp", yl, ytmbuf[ti:ti + 128, :], rd=["ytmb"], wr=["Byl"])
            zl = c.B[:, 2048:4096]
            P.dma("sp", zl, zbuf[ti:ti + 128, :], rd=["zb"], wr=["Bz"])
            yg = c.B[:, 4096:6144]
            rs = c.GR[0:1, 0:1] if False else c.ET[0][:, 0:1]
            for piece in range(4):
                P.tt(yg[:, piece * 512:(piece + 1) * 512], yl[:, piece * 512:(piece + 1) * 512],
                     zl[:, piece * 512:(piece + 1) * 512], ALU.mult,
                     rd=["Byl", "Bz"], wr=["Byg"])
            for piece in range(4):
                sq = c.EB[piece % 2][:]
                P.act(sq, yg[:, piece * 512:(piece + 1) * 512], AF.Square,
                      rd=["Byg"], wr=[f"EB{piece%2}"])
                col = c.ET[0][:, 1 + piece:2 + piece]
                P.red(col, sq, ALU.add, rd=[f"EB{piece%2}"], wr=["ET0"])
            P.tt(c.ET[0][:, 1:2], c.ET[0][:, 1:2], c.ET[0][:, 2:3], ALU.add, rd=["ET0"], wr=["ET0"])
            P.tt(c.ET[0][:, 3:4], c.ET[0][:, 3:4], c.ET[0][:, 4:5], ALU.add, rd=["ET0"], wr=["ET0"])
            P.tt(c.ET[0][:, 0:1], c.ET[0][:, 1:2], c.ET[0][:, 3:4], ALU.add, rd=["ET0"], wr=["ET0"])
            P.ts(c.ET[0][:, 0:1], c.ET[0][:, 0:1], 1.0 / 2048.0, None, ALU.mult, rd=["ET0"], wr=["ET0"])
            epsc, epsk2 = c.ccol(1e-5)
            P.act(c.ET[0][:, 0:1], c.ET[0][:, 0:1], AF.Ln, bias=epsc, rd=["ET0", epsk2], wr=["ET0"])
            P.act(c.ET[0][:, 0:1], c.ET[0][:, 0:1], AF.Exp, scale=-0.5, rd=["ET0"], wr=["ET0"])
            for piece in range(4):
                P.ts(yg[:, piece * 512:(piece + 1) * 512], yg[:, piece * 512:(piece + 1) * 512],
                     c.ET[0][:, 0:1], None, ALU.mult, rd=["Byg", "ET0"], wr=["Byg"])
                P.tt(yg[:, piece * 512:(piece + 1) * 512], yg[:, piece * 512:(piece + 1) * 512],
                     nwbf[:, piece * 512:(piece + 1) * 512], ALU.mult,
                     rd=["Byg", "Bnw"], wr=["Byg"])
            for ch in range(16):
                P.trans(c.PTB[:, 0:128], yg[:, ch * 128:(ch + 1) * 128], c.s_idb[:],
                        rd=["Byg", "c4"], wr=["PTB"])
                P.copy(ygc[:, ch * 512 + tt * 128: ch * 512 + (tt + 1) * 128], c.PTB[:, 0:128],
                       rd=["PTB"], wr=["Bygc"])
        for dk in range(KT):
            pb = c.PS[2 + dk % 2]
            pk = f"PS{2 + dk % 2}"
            for f in range(16):
                warena = c.W1 if f < 8 else c.W2
                wkk = ["W1g0", "W1g1"] if f < 8 else ["W2g0", "W2g1"]
                fo = f % 8
                P.mm(pb[:], warena[:, fo * 1024 + dk * 128: fo * 1024 + dk * 128 + 128],
                     ygc[:, f * 512:(f + 1) * 512], start=(f == 0), stop=(f == 15),
                     rd=wkk + ["Bygc"], wr=[pk])
            et1 = c.ET[1][:]
            P.dma("sp", et1, xin[dk * 128:(dk + 1) * 128, t0:t0 + TS], rd=["xres"], wr=["ET1"])
            ebn = c.EB[0][:]
            P.dma("sp", ebn, ypbuf[dk * 128:(dk + 1) * 128, t0:t0 + TS], rd=["yhbmp"], wr=["EB0"])
            P.tt(et1, et1, ebn, ALU.add, rd=["EB0", "ET1"], wr=["ET1"])
            P.tt(et1, et1, pb[:], ALU.add, rd=[pk, "ET1"], wr=["ET1"])
            P.dma("sp", xout[dk * 128:(dk + 1) * 128, t0:t0 + TS], et1, rd=["ET1"], wr=["xres"])
    P.fence("sp", ["xres"])


def vec_router(w):
    w = np.asarray(w, np.float32)
    if w.shape[0] != 1024:
        w = w.T
    ne = w.shape[1]
    return np.ascontiguousarray(np.concatenate([w[k * 128:(k + 1) * 128, :] for k in range(8)], axis=1))


def vec_pm(v):
    v = np.asarray(v, np.float32)
    return np.ascontiguousarray(v.reshape(-1, 128).T)


def prep_weights(inputs):
    W = {}
    t = lambda a: np.ascontiguousarray(np.asarray(a).T)
    W["L1_ffn_wgT"] = t(inputs["L1_ffn_w_gate"])
    W["L1_ffn_wuT"] = t(inputs["L1_ffn_w_up"])
    W["L1_ffn_wdT"] = t(inputs["L1_ffn_w_down"])
    W["L2_ffn_w1T"] = t(inputs["L2_ffn_w1"])
    W["L2_ffn_w2T"] = t(inputs["L2_ffn_w2"])
    W["L2_ffn_b1c"] = vec_pm(inputs["L2_ffn_b1"])
    W["L2_ffn_b2c"] = vec_pm(inputs["L2_ffn_b2"])
    for n in ("Wa", "Wb", "Wc", "W1", "W2"):
        W["L0_attn_" + n] = np.ascontiguousarray(np.asarray(inputs["L0_attn_" + n], np.float32))
    W["L0_attn_vec"] = np.hstack([vec_pm(inputs["L0_attn_" + n])
                                  for n in ("g1", "b1", "ba", "bb", "bc", "g2", "b2", "bm2")])
    W["L0_attn_bm1"] = vec_pm(inputs["L0_attn_bm1"])
    for n in ("Wq", "Wk", "Wv", "Wo"):
        W["L1_attn_" + n + "T"] = t(inputs["L1_attn_" + n])
    W["L1_attn_vec"] = np.hstack([vec_pm(inputs["L1_attn_" + n]) for n in ("bq", "bk", "bv", "bo")])
    W["L0_ffn_W1"] = np.ascontiguousarray(np.asarray(inputs["L0_ffn_W1"], np.float32))
    W["L0_ffn_W2"] = np.ascontiguousarray(np.asarray(inputs["L0_ffn_W2"], np.float32))
    W["L0_ffn_WrT"] = vec_router(inputs["L0_ffn_Wr"])
    W["L0_ffn_br_row"] = np.asarray(inputs["L0_ffn_br"], np.float32).reshape(1, -1)
    W["L0_ffn_b1c"] = np.hstack([vec_pm(inputs["L0_ffn_b1"][e]) for e in range(8)])
    W["L0_ffn_b2n"] = np.ascontiguousarray(np.asarray(inputs["L0_ffn_b2"], np.float32))
    W["L3_ffn_wgT"] = np.ascontiguousarray(np.transpose(np.asarray(inputs["L3_ffn_w_gate"], np.float32), (0, 2, 1)))
    W["L3_ffn_wuT"] = np.ascontiguousarray(np.transpose(np.asarray(inputs["L3_ffn_w_up"], np.float32), (0, 2, 1)))
    W["L3_ffn_wdT"] = np.ascontiguousarray(np.transpose(np.asarray(inputs["L3_ffn_w_down"], np.float32), (0, 2, 1)))
    W["L3_ffn_sgT"] = t(inputs["L3_ffn_shared_gate_w"])
    W["L3_ffn_suT"] = t(inputs["L3_ffn_shared_up_w"])
    W["L3_ffn_sdT"] = t(inputs["L3_ffn_shared_down_w"])
    W["L3_ffn_gwT"] = vec_router(t(inputs["L3_ffn_gate_w"]).T)
    W["L3_ffn_ebias_row"] = np.asarray(inputs["L3_ffn_expert_bias"], np.float32).reshape(1, -1)
    W["L3_attn_qkv_w"] = np.ascontiguousarray(np.asarray(inputs["L3_attn_qkv_w"], np.float32))
    qb = np.asarray(inputs["L3_attn_qkv_b"], np.float32)
    W["L3_attn_bq"] = vec_pm(qb[0:1024])
    W["L3_attn_bk"] = vec_pm(qb[1024:2048])
    W["L3_attn_bv"] = vec_pm(qb[2048:3072])
    w1w = np.asarray(inputs["L3_attn_w1_write_w"], np.float32)
    projs = [w1w[0:1024], w1w[1024:2048],
             np.asarray(inputs["L3_attn_w2_write_w"], np.float32),
             np.asarray(inputs["L3_attn_w1_read_w"], np.float32),
             np.asarray(inputs["L3_attn_w2_read_w"], np.float32)]
    W["L3_attn_wpts"] = np.concatenate(
        [pr[k * 128:(k + 1) * 128, :] for pr in projs for k in range(8)], axis=1)
    W["L3_attn_mem_value_w"] = np.ascontiguousarray(np.asarray(inputs["L3_attn_mem_value_w"], np.float32))
    W["L3_attn_mvb"] = vec_pm(inputs["L3_attn_mem_value_b"])
    W["L3_attn_mgw"] = vec_router(np.asarray(inputs["L3_attn_mem_gate_w"], np.float32))
    W["L3_attn_out_w"] = np.ascontiguousarray(np.asarray(inputs["L3_attn_out_w"], np.float32))
    W["L3_attn_owb"] = vec_pm(inputs["L3_attn_out_b"])
    r = np.zeros((1, 128), np.float32)
    r[0, 0:16] = np.asarray(inputs["L3_attn_mem_gate_b"], np.float32)
    r[0, 16:32] = np.asarray(inputs["L3_attn_mem_scale"], np.float32)
    r[0, 32:48] = np.asarray(inputs["L3_attn_decay_logits"], np.float32)
    r[0, 48:84] = np.asarray(inputs["L3_attn_J"], np.float32).reshape(-1)
    W["L3_attn_rows"] = r
    io = np.zeros((128, 2), np.float32)
    io[:, 0] = np.arange(128) + 1.0
    io[:, 1] = 127.0 - np.arange(128)
    W["c_iota1"] = io
    a = np.arange(128, dtype=np.float32)
    W["c_dif"] = np.ascontiguousarray(a[:, None] - a[None, :])
    W["c_sutri"] = np.ascontiguousarray((a[:, None] <= a[None, :]).astype(np.float32))
    W["L2_attn_ipWT"] = t(inputs["L2_attn_in_proj_w"])
    W["L2_attn_opWT"] = t(inputs["L2_attn_out_proj_w"])
    W["L2_attn_lnp"] = np.hstack([vec_pm(inputs["L2_attn_ln_w"]), vec_pm(inputs["L2_attn_ln_b"])])
    cwv = np.asarray(inputs["L2_attn_conv_w"], np.float32)[:, 0, :]
    convc = np.zeros((128, 85), np.float32)
    for ch in range(17):
        convc[:, ch * 4:(ch + 1) * 4] = cwv[ch * 128:(ch + 1) * 128, :]
    convc[:, 68:85] = vec_pm(inputs["L2_attn_conv_b"])
    W["L2_attn_convc"] = convc
    W["L2_attn_dtb"] = np.asarray(inputs["L2_attn_dt_bias"], np.float32).reshape(32, 1)
    W["L2_attn_alog"] = np.asarray(inputs["L2_attn_A_log"], np.float32).reshape(32, 1)
    W["L2_attn_drow"] = np.asarray(inputs["L2_attn_D_skip"], np.float32).reshape(1, 32)
    W["L2_attn_nrow"] = np.asarray(inputs["L2_attn_norm_w"], np.float32).reshape(1, 2048)
    W["c_srep32"] = np.kron(np.eye(32, dtype=np.float32), np.ones((1, 64), np.float32))
    W["c_srepi"] = np.tile(np.eye(64, dtype=np.float32), (1, 32))
    t64 = (np.arange(64)[:, None] <= np.arange(64)[None, :]).astype(np.float32)
    W["c_triuc"] = np.ascontiguousarray(np.tile(t64, (1, 32)))
    W["c_idf32"] = np.eye(32, dtype=np.float32)
    return W


SUBLAYERS = {}


def build_program(W, sublayer_list):
    nc = bass.Bass("TRN2", target_bir_lowering=False, debug=False)
    P = Prog(nc)
    c = make_ctx(nc, P, base_consts())
    xin = nc.dram_tensor("xin", [D, T], F32, kind="ExternalInput").ap()
    xout = nc.dram_tensor("xout", [D, T], F32, kind="ExternalOutput").ap()
    xa = nc.dram_tensor("xa", [D, T], F32, kind="Internal").ap()
    xb = nc.dram_tensor("xb", [D, T], F32, kind="Internal").ap()
    bufs = [xa, xb]
    n = len(sublayer_list)
    cur = xin
    for i, name in enumerate(sublayer_list):
        if i:
            P.mega_barrier()
            P.new_epoch()
        dst = xout if i == n - 1 else bufs[i % 2]
        SUBLAYERS[name](P, c, W, cur, dst, i)
        cur = dst
    nc_final = P.finalize()
    host = {k: v[0] for k, v in P.host_inputs.items()}
    host["__xin__"] = None
    return nc_final, host


def emit_dbg_ln(P, c, W, xin, xout, uid):
    tag = f"dbg{uid}_"
    xsf = b_f32(c, 0, KT * TS)
    for s in range(NSLC):
        t0 = s * TS
        tiles = [acol(c, k, t0, TS) for k in range(KT)]
        keys = [f"A{k}" for k in range(KT)]
        emit_ln_slice(P, c, xin, t0, TS, xsf, ["B0", "B1"], tiles, keys, tag + f"ln{s}_")
        for k in range(KT):
            e32 = c.ET[k % 2][:]
            P.copy(e32, tiles[k], rd=[keys[k]], wr=[f"ET{k%2}"])
            P.dma("sp", xout[k * 128:(k + 1) * 128, t0:t0 + TS], e32, rd=[f"ET{k%2}"], wr=[f"xo{k}_{s}"])
    P.fence("sp", [f"xo{k}_{s}" for k in range(KT) for s in range(NSLC)])


SUBLAYERS["dbg_ln"] = emit_dbg_ln
SUBLAYERS["l1ffn"] = emit_l1ffn
SUBLAYERS["l2ffn"] = emit_l2ffn
SUBLAYERS["l0attn"] = emit_l0attn
SUBLAYERS["l1attn"] = emit_l1attn
SUBLAYERS["l0ffn"] = emit_l0ffn
SUBLAYERS["l3ffn"] = emit_l3ffn
SUBLAYERS["l3attn"] = emit_l3attn
SUBLAYERS["l2attn"] = emit_l2attn


def emit_dbg_route(P, c, W, xin, xout, uid):
    tag = f"dbgr{uid}_"
    Wr = P.input("L0_ffn_WrT", W["L0_ffn_WrT"])
    br = c.sbuf_param(P, "L0_ffn_br", P.input("L0_ffn_br", W["L0_ffn_br_row"]), [1, 8])
    wrs = c.sbuf_param(P, "L0_ffn_Wr", Wr, [128, 64], dtype=BF16)
    xsf = b_f32(c, 0, KT * TS)
    NE = 8
    for s in range(NSLC):
        t0 = s * TS
        ytiles = [c.B[:, 2 * T + k * 512: 2 * T + k * 512 + 512] for k in range(KT)]
        ykeys = ["B2"] * KT
        emit_ln_slice(P, c, xin, t0, TS, xsf, ["B0", "B1"], ytiles, ykeys, tag + f"ln{s}_")
        emit_routing(P, c, W, ytiles, ykeys, s, tag, NE,
                     lambda k: wrs[:, k * 8:(k + 1) * 8], ["pL0_ffn_Wr"],
                     br, ["pL0_ffn_br"])
    comb_cm = emit_comb_cm(P, c, NE, tag)
    for s in range(NSLC):
        t0 = s * TS
        e32 = c.ET[0][:]
        P.memset(e32, 0.0, wr=["ET0"], q="dve")
        P.copy(e32[0:8, :], comb_cm[:, t0:t0 + TS], rd=["A0", "A1"], wr=["ET0"])
        P.dma("sp", xout[0:128, t0:t0 + TS], e32, rd=["ET0"], wr=["xres"])
        for k in range(2, KT):
            P.dma("sp", xout[k * 128:(k + 1) * 128, t0:t0 + TS], e32, rd=["ET0"], wr=["xres"])
    e32 = c.ET[0][:]
    P.memset(e32, 0.0, wr=["ET0"], q="dve")
    P.copy(e32[:, 0:256], c.CMB[:, 0:256], rd=["CMB"], wr=["ET0"])
    P.dma("sp", xout[128:256, 0:512], e32, rd=["ET0"], wr=["xres"])
    P.fence("sp", ["xres"])


SUBLAYERS["dbg_route"] = emit_dbg_route

SUB_ORDER = ["l0ffn", "l0attn", "l0ffn", "l1ffn", "l1attn", "l1ffn",
             "l2ffn", "l2attn", "l2ffn", "l3ffn", "l3attn", "l3ffn"]

FUSED = False


def kernel(**inputs):
    W = prep_weights(inputs)
    x = np.asarray(inputs["L0_attn_x"])
    xs = [np.ascontiguousarray(x[core].T) for core in range(NCORE)]
    if FUSED:
        nc, host = build_program(W, SUB_ORDER)
        in_maps = []
        for core in range(NCORE):
            m = {k: v for k, v in host.items() if k != "__xin__"}
            m["xin"] = xs[core]
            in_maps.append(m)
        res = run_bass_kernel_spmd(nc, in_maps, core_ids=list(range(NCORE)))
        return np.stack([np.ascontiguousarray(r["xout"].T) for r in res.results])
    progs = {}
    for name in SUB_ORDER:
        if name not in progs:
            progs[name] = build_program(W, [name])
    for name in SUB_ORDER:
        nc, host = progs[name]
        in_maps = []
        for core in range(NCORE):
            m = {k: v for k, v in host.items() if k != "__xin__"}
            m["xin"] = xs[core]
            in_maps.append(m)
        res = run_bass_kernel_spmd(nc, in_maps, core_ids=list(range(NCORE)))
        xs = [np.ascontiguousarray(r["xout"]) for r in res.results]
    return np.stack([np.ascontiguousarray(xc.T) for xc in xs])
